# version 09
# baseline (speedup 1.0000x reference)
import math
import numpy as np
import ml_dtypes
import concourse.bass as bass
import concourse.mybir as mybir
from concourse.bass_utils import run_bass_kernel_spmd

F32 = mybir.dt.float32
BF16 = mybir.dt.bfloat16
F16 = mybir.dt.float16
F8 = mybir.dt.float8e4
I8 = mybir.dt.int8
AF = mybir.ActivationFunctionType
ALU = mybir.AluOpType
AX = mybir.AxisListType

TOK = 2048
D = 2048
INNER = 4096
NT = 16
ND = 16
NCH = 32
RND_C = float(np.float32(1.5 * 2.0 ** 23))
INV127 = float(np.float32(1.0 / 127.0))
SM_SCALE = 1.0 / float(np.float32(math.sqrt(np.float32(128.0))))
N_SUB = 6
DBG = False


class Hd:
    __slots__ = ("sem", "val", "small")
    def __init__(self, sem, val, small=False):
        self.sem, self.val, self.small = sem, val, small


class Buf:
    __slots__ = ("w", "r")
    def __init__(self):
        self.w = []
        self.r = []
    def reset(self):
        self.w = []
        self.r = []


class P:
    def __init__(self, nc):
        self.nc = nc
        self._stack = []
        self.sems = {}
        self.cnt = {}
        self.last_wait = {}
        for name in ("T", "V", "S", "G", "SY", "GD", "SD"):
            self.sems[name] = self._enter(nc.semaphore(f"sem_{name}"))
            self.cnt[name] = 0
        self.eng = {"T": nc.tensor, "V": nc.vector, "S": nc.scalar,
                    "G": nc.gpsimd, "SY": nc.sync}
        self.own = {"T": self.sems["T"], "V": self.sems["V"],
                    "S": self.sems["S"], "G": self.sems["G"]}
        self.dmaq = {"SY": ("SY", "SY"), "G": ("G", "GD"), "S": ("S", "SD")}
        self.last_drain = {"V": 0, "S": 0}
        self.flush_cover = {"V": 0, "S": 0}
        self.flush_val = {"V": 0, "S": 0}

    def _enter(self, cm):
        v = cm.__enter__()
        self._stack.append(cm)
        return v

    def _flush(self, pname):
        cover = self.cnt[pname]
        r = self.eng[pname].drain()
        self.cnt[pname] += 1
        r.then_inc(self.sems[pname], 1)
        self.flush_cover[pname] = cover
        self.flush_val[pname] = self.cnt[pname]
        self.last_drain[pname] = cover

    def _wait(self, ename, deps):
        eng = self.eng[ename]
        own = self.own.get(ename)
        need_drain = False
        for h in deps:
            if h is None:
                continue
            if h.sem is own:
                if ename in ("V", "S") and h.val > self.last_drain[ename]:
                    need_drain = True
                continue
            val = h.val
            if h.small:
                for pn in ("V", "S"):
                    if h.sem is self.sems[pn]:
                        if h.val > self.flush_cover[pn]:
                            self._flush(pn)
                        val = max(val, self.flush_val[pn])
                        break
            key = (ename, h.sem.name)
            if self.last_wait.get(key, -1) >= val:
                continue
            eng.wait_ge(h.sem, val)
            self.last_wait[key] = val
        if need_drain:
            eng.drain()
            self.last_drain[ename] = self.cnt[ename]

    def _deps(self, reads, writes):
        deps = []
        for b in reads:
            deps.extend(b.w)
        for b in writes:
            deps.extend(b.w)
            deps.extend(b.r)
        return deps

    @staticmethod
    def _is_small(args):
        try:
            shp = args[0].shape
            n = 1
            for d in shp:
                n *= int(d)
            return n <= 4096
        except Exception:
            return True

    def op(self, ename, fn, *args, reads=(), writes=(), extra=(), **kw):
        deps = self._deps(reads, writes) + list(extra)
        self._wait(ename, deps)
        r = fn(*args, **kw)
        self.cnt[ename] += 1
        h = Hd(self.sems[ename], self.cnt[ename], self._is_small(args))
        r.then_inc(h.sem, 1)
        for b in reads:
            b.r.append(h)
        for b in writes:
            b.w = [h]
            b.r = []
        return h

    def upd(self, ename, fn, *args, reads=(), writes=(), extra=(), **kw):
        deps = self._deps(reads, ()) + list(extra)
        for b in writes:
            deps.extend(b.r)
        self._wait(ename, deps)
        r = fn(*args, **kw)
        self.cnt[ename] += 1
        h = Hd(self.sems[ename], self.cnt[ename], self._is_small(args))
        r.then_inc(h.sem, 1)
        for b in reads:
            b.r.append(h)
        for b in writes:
            b.w.append(h)
        return h

    def dma(self, q, out_ap, in_ap, reads=(), writes=(), extra=(), accum=False):
        ename, sname = self.dmaq[q]
        deps = self._deps(reads, writes) + list(extra)
        if self.cnt[sname]:
            deps.append(Hd(self.sems[sname], self.cnt[sname]))
        self._wait(ename, deps)
        r = self.eng[ename].dma_start(out_ap, in_ap)
        self.cnt[sname] += 16
        h = Hd(self.sems[sname], self.cnt[sname])
        r.then_inc(h.sem, 16)
        for b in reads:
            b.r.append(h)
        for b in writes:
            b.w = [h]
            b.r = []
        return h

    def barrier(self):
        snap = dict(self.cnt)
        for ename in ("T", "V", "S", "G", "SY"):
            eng = self.eng[ename]
            own = self.own.get(ename)
            for sname, val in snap.items():
                sem = self.sems[sname]
                if sem is own or val == 0:
                    continue
                key = (ename, sem.name)
                if self.last_wait.get(key, -1) >= val:
                    continue
                eng.wait_ge(sem, val)
                self.last_wait[key] = val
        if snap["V"] > self.last_drain["V"]:
            self.eng["V"].drain()
            self.last_drain["V"] = snap["V"]
        if snap["S"] > self.last_drain["S"]:
            self.eng["S"].drain()
            self.last_drain["S"] = snap["S"]

    def finish(self, handles):
        self._wait("SY", handles)

    def close(self):
        for cm in reversed(self._stack):
            cm.__exit__(None, None, None)


def build_nc(n_sub=N_SUB):
    nc = bass.Bass("TRN2", target_bir_lowering=False)
    p = P(nc)
    T, V, S, G = nc.tensor, nc.vector, nc.scalar, nc.gpsimd
    from contextlib import ExitStack

    def din(name, shape, dt):
        return nc.dram_tensor(name, shape, dt, kind="ExternalInput").ap()

    x_in = din("x", [TOK, D], F32)
    y_out = nc.dram_tensor("y", [TOK, D], F32, kind="ExternalOutput").ap()
    xres = nc.dram_tensor("xres", [TOK, D], F32, kind="Internal").ap()
    xscr = nc.dram_tensor("xscr", [INNER, TOK], F32, kind="Internal").ap()
    combscr = nc.dram_tensor("combscr", [INNER, TOK], BF16, kind="Internal").ap()
    cc_in = [nc.dram_tensor(f"cc_in{i}", [1, 4], F32, kind="Internal").ap() for i in range(2)]
    cc_out = [nc.dram_tensor(f"cc_out{i}", [1, 4], F32, kind="Internal",
                             addr_space="Shared").ap() for i in range(2)]

    w0t = din("w0t", [D, D], I8)
    w0sc_bc = din("w0sc_bc", [128, D], F32)
    b0_bc = din("b0_bc", [128, D], F32)
    wint = din("wint", [NCH * 128, 2048], F32)
    wgt = din("wgt", [NCH * 128, 2048], F32)
    wot = din("wot", [INNER, D], F32)
    convw = din("convw", [128, NCH * 4], F32)
    convb = din("convb", [128, NCH], F32)
    dtw_sel = din("dtw_sel", [128, 64], F32)
    dtb_col = din("dtb_col", [64, 1], F32)
    alog_t = din("alog_t", [128, NCH], F32)
    b_t = din("b_t", [128, NCH], F32)
    c_t = din("c_t", [128, NCH], F32)
    d_t = din("d_t", [128, NCH], F32)
    onesel2 = din("onesel2", [2, 128], F32)
    ones1 = din("ones1", [1, 128], F32)
    ident_bf = din("ident_bf", [128, 128], BF16)
    ident_f32 = din("ident_f32", [128, 128], F32)
    maskadd = din("maskadd", [128, 128], F32)
    sel128 = din("sel128", [128, 512], BF16)
    wq_t = din("wq_t", [16 * 128, 2048], F32)
    wk_t = din("wk_t", [4 * 128, 2048], F32)
    wv_t = din("wv_t", [4 * 128, 2048], F32)
    wo = din("wo", [D, D], F32)
    bq_t = din("bq_t", [128, 16], F32)
    bk_t = din("bk_t", [128, 4], F32)
    bv_t = din("bv_t", [128, 4], F32)
    bo_bc = din("bo_bc", [128, D], F32)
    w1t = din("w1t", [D, D], BF16)
    b1_bc = din("b1_bc", [128, D], BF16)

    st_handles = []
    dbg = {}
    if DBG:
        for nm, sh, dt in [("d_ln", [128, 2048], F32), ("d_qbf", [128, 2048], BF16),
                           ("d_qfm", [128, 2048], BF16), ("d_sm", [128, 64], F32),
                           ("d_st", [128, 64], F32), ("d_po", [128, 2048], F32),
                           ("d_row", [1, 128], F32), ("d_xsws", [128, 8], F32), ("d_cc", [1, 4], F32),
                           ("d_q8", [128, 2048], BF16), ("d_w8", [128, 2048], BF16), ("d_po1", [128, 2048], F32)]:
            dbg[nm] = nc.dram_tensor(nm, sh, dt, kind="ExternalOutput").ap()

    with ExitStack() as ST:
        def sb(name, shape, dt):
            return ST.enter_context(nc.sbuf_tensor(name, shape, dt))

        fm = sb("fm", [128, ND * TOK], BF16)
        fmb = [Buf() for _ in range(ND)]
        stats = sb("stats", [128, 64], F32)
        stats_b = Buf()
        smalls = sb("smalls", [128, 64], F32)
        smalls_b = Buf()
        consts = sb("consts", [128, 4 * NCH + NCH], F32)
        consts_b = Buf()
        mpar = sb("mpar", [128, 3 * NCH], F32)
        mpar_b = Buf()
        dtwsel = sb("dtwsel", [128, 64], F32)
        dtwsel_b = Buf()
        dtbs = sb("dtbs", [64, 1], F32)
        dtbs_b = Buf()
        osel = sb("osel", [2, 128], F32)
        osel_b = Buf()
        ones1s = sb("ones1s", [1, 128], F32)
        ones1s_b = Buf()
        identb = sb("identb", [128, 128], BF16)
        identb_b = Buf()
        identf = sb("identf", [128, 128], F32)
        identf_b = Buf()
        masks = sb("masks", [128, 128], F32)
        masks_b = Buf()
        sel128s = sb("sel128s", [128, 512], BF16)
        sel128s_b = Buf()
        biasq = sb("biasq", [128, 24], F32)
        biasq_b = Buf()
        row1 = sb("row1", [1, 128], F32)
        row1_b = Buf()
        rowt = sb("rowt", [1, 128], F32)
        rowt_b = Buf()
        one1 = sb("one1", [1, 4], F32)
        one1_b = Buf()
        xsws = sb("xsws", [128, 8], F32)
        xsws_b = Buf()

        p.dma("SY", consts[:, 0:4 * NCH], convw, writes=[consts_b])
        p.dma("SY", consts[:, 4 * NCH:5 * NCH], convb, extra=p._deps([consts_b], ()), reads=[consts_b])
        p.dma("SY", dtwsel[:, :], dtw_sel, writes=[dtwsel_b])
        p.dma("SY", dtbs[:, :], dtb_col, writes=[dtbs_b])
        p.dma("SY", osel[:, :], onesel2, writes=[osel_b])
        p.dma("SY", ones1s[:, :], ones1, writes=[ones1s_b])
        p.dma("SY", identb[:, :], ident_bf, writes=[identb_b])
        p.dma("SY", identf[:, :], ident_f32, writes=[identf_b])
        p.dma("SY", masks[:, :], maskadd, writes=[masks_b])
        p.dma("SY", sel128s[:, :], sel128, writes=[sel128s_b])
        p.dma("G", biasq[:, 0:16], bq_t, writes=[biasq_b])
        p.dma("G", biasq[:, 16:20], bk_t, reads=[biasq_b])
        p.dma("G", biasq[:, 20:24], bv_t, reads=[biasq_b])
        p.dma("G", mpar[:, 0:NCH], alog_t, writes=[mpar_b])
        p.dma("G", mpar[:, NCH:2 * NCH], b_t, reads=[mpar_b])
        p.dma("G", mpar[:, 2 * NCH:3 * NCH], c_t, reads=[mpar_b])
        p.op("S", S.activation, mpar[:, 0:NCH], mpar[:, 0:NCH], AF.Exp,
             reads=[mpar_b], writes=[mpar_b])
        p.op("V", V.tensor_scalar, mpar[:, 0:NCH], mpar[:, 0:NCH], -1.0, None, ALU.mult,
             reads=[mpar_b], writes=[mpar_b])
        p.op("V", V.tensor_mul, mpar[:, NCH:2 * NCH], mpar[:, NCH:2 * NCH],
             mpar[:, 2 * NCH:3 * NCH], reads=[mpar_b], writes=[mpar_b])
        p.dma("G", mpar[:, 2 * NCH:3 * NCH], d_t, reads=[mpar_b])

        def stats_tile(i, xa_ap, xa_b, scr_bf, scr_bf_b):
            p.upd("V", V.reduce_sum, stats[:, 32 + i:33 + i], xa_ap, AX.X,
                  reads=[xa_b], writes=[stats_b])
            p.upd("S", S.activation, scr_bf, xa_ap, AF.Square,
                  accum_out=stats[:, 48 + i:49 + i],
                  reads=[xa_b], writes=[stats_b, scr_bf_b])
            p.upd("V", V.tensor_scalar, stats[:, 32 + i:33 + i], stats[:, 32 + i:33 + i],
                  1.0 / D, None, ALU.mult, reads=[stats_b], writes=[stats_b])
            p.upd("V", V.tensor_scalar, stats[:, 48 + i:49 + i], stats[:, 48 + i:49 + i],
                  1.0 / D, None, ALU.mult, reads=[stats_b], writes=[stats_b])
            p.upd("V", V.scalar_tensor_tensor, stats[:, 0 + i:1 + i], stats[:, 32 + i:33 + i],
                  -1.0, stats[:, 32 + i:33 + i], ALU.mult, ALU.mult,
                  reads=[stats_b], writes=[stats_b])
            p.upd("V", V.tensor_add, stats[:, 48 + i:49 + i], stats[:, 48 + i:49 + i],
                  stats[:, 0 + i:1 + i], reads=[stats_b], writes=[stats_b])
            p.upd("V", V.tensor_scalar, stats[:, 48 + i:49 + i], stats[:, 48 + i:49 + i],
                  1e-5, None, ALU.add, reads=[stats_b], writes=[stats_b])
            p.upd("S", S.activation, stats[:, 48 + i:49 + i], stats[:, 48 + i:49 + i],
                  AF.Ln, reads=[stats_b], writes=[stats_b])
            p.upd("S", S.activation, stats[:, 0 + i:1 + i], stats[:, 48 + i:49 + i],
                  AF.Exp, scale=-0.5, reads=[stats_b], writes=[stats_b])
            p.upd("V", V.scalar_tensor_tensor, stats[:, 16 + i:17 + i], stats[:, 32 + i:33 + i],
                  -1.0, stats[:, 0 + i:1 + i], ALU.mult, ALU.mult,
                  reads=[stats_b], writes=[stats_b])

        def ln_apply(i, out_ap, out_b, xa_ap, xa_b):
            p.op("S", S.activation, out_ap, xa_ap, AF.Identity,
                 bias=stats[:, 16 + i:17 + i], scale=stats[:, 0 + i:1 + i],
                 reads=[xa_b, stats_b], writes=[out_b])

        def colmax_to_row1(col_ap, col_b, psc, psc_b):
            p.op("T", T.matmul, psc[0:1, 0:128], col_ap, identf[:, :],
                 start=True, stop=True, reads=[col_b, identf_b], writes=[psc_b])
            p.op("V", V.tensor_copy, rowt[:, :], psc[0:1, 0:128],
                 reads=[psc_b], writes=[rowt_b])
            p.upd("V", V.reduce_max, row1[0:1, 0:1], rowt[:, :], AX.X,
                  reads=[rowt_b], writes=[row1_b])

        def bcast_scalar(src_ap, src_b, dst_col, psc, psc_b):
            p.op("T", T.matmul, psc[:, 0:1], ones1s[:, :], src_ap,
                 start=True, stop=True, reads=[src_b, ones1s_b], writes=[psc_b])
            p.upd("V", V.tensor_copy, xsws[:, dst_col:dst_col + 1], psc[:, 0:1],
                  reads=[psc_b], writes=[xsws_b])

        def src_ap(s, i):
            return (x_in if s == 0 else xres)[128 * i:128 * (i + 1), :]

        def dst_ap(s, i):
            return (y_out if s == n_sub - 1 else xres)[128 * i:128 * (i + 1), :]

        xres_b = [Buf() for _ in range(NT)]

        def build_ffn0(s):
            with ExitStack() as E:
                def lsb(name, shape, dt):
                    return E.enter_context(nc.sbuf_tensor(name, shape, dt))
                xa = [lsb(f"ffn0_xa{u}_{s}", [128, 2048], F32) for u in range(2)]
                xa_b = [Buf(), Buf()]
                ln32 = [lsb(f"ffn0_ln{u}_{s}", [128, 2048], F32) for u in range(2)]
                ln32_b = [Buf(), Buf()]
                qbf = [lsb(f"ffn0_qbf{u}_{s}", [128, 2048], BF16) for u in range(2)]
                qbf_b = [Buf(), Buf()]
                qfm = [lsb(f"ffn0_qfm{u}_{s}", [128, 2048], BF16) for u in range(2)]
                qfm_b = [Buf(), Buf()]
                i8s = lsb(f"ffn0_i8_{s}", [128, 2048], I8)
                i8s_b = Buf()
                bcs = [lsb(f"ffn0_bc{u}_{s}", [128, 2048], F32) for u in range(2)]
                bcs_b = [Buf(), Buf()]
                ef32 = [lsb(f"ffn0_e32{u}_{s}", [128, 512], F32) for u in range(2)]
                ef32_b = [Buf(), Buf()]
                ef16 = [lsb(f"ffn0_e16{u}_{s}", [128, 512], F16) for u in range(2)]
                ef16_b = [Buf(), Buf()]
                smq = [lsb(f"ffn0_smq{u}_{s}", [128, 4], F32) for u in range(2)]
                smq_b = [Buf(), Buf()]
                pt = E.enter_context(nc.psum_tensor(f"ffn0_pt_{s}", [128, 1024], F32))
                ptb2 = [Buf(), Buf()]
                po = E.enter_context(nc.psum_tensor(f"ffn0_po_{s}", [128, 2048], F32))
                po_b = [Buf() for _ in range(4)]
                for j in range(ND):
                    p.dma("SY", i8s[:, :], w0t[128 * j:128 * (j + 1), :], writes=[i8s_b])
                    p.op("V", V.tensor_copy, fm[:, TOK * j:TOK * (j + 1)], i8s[:, :],
                         reads=[i8s_b], writes=[fmb[j]])
                p.dma("SY", bcs[0][:, :], w0sc_bc, writes=[bcs_b[0]])
                p.dma("SY", bcs[1][:, :], b0_bc, writes=[bcs_b[1]])
                for i in range(NT):
                    u = i % 2
                    pr = i % 2
                    p.dma("G", xa[u][:, :], src_ap(s, i), reads=[xres_b[i]], writes=[xa_b[u]])
                    stats_tile(i, xa[u][:, :], xa_b[u], ln32[pr][:, :], ln32_b[pr])
                    ln_apply(i, ln32[pr][:, :], ln32_b[pr], xa[u][:, :], xa_b[u])
                    p.op("V", V.reduce_max, smq[pr][:, 0:1], ln32[pr][:, :], AX.X, True,
                         reads=[ln32_b[pr]], writes=[smq_b[pr]])
                    p.op("V", V.tensor_scalar, smq[pr][:, 1:2], smq[pr][:, 0:1], 1e-7, None,
                         ALU.max, reads=[smq_b[pr]], writes=[smq_b[pr]])
                    p.op("V", V.tensor_scalar, smq[pr][:, 1:2], smq[pr][:, 1:2], INV127, None,
                         ALU.mult, reads=[smq_b[pr]], writes=[smq_b[pr]])
                    p.op("V", V.reciprocal, smq[pr][:, 2:3], smq[pr][:, 1:2],
                         reads=[smq_b[pr]], writes=[smq_b[pr]])
                    p.op("V", V.tensor_scalar, ln32[pr][:, :], ln32[pr][:, :],
                         smq[pr][:, 2:3], None,
                         ALU.mult, reads=[ln32_b[pr], smq_b[pr]], writes=[ln32_b[pr]])
                    p.op("V", V.tensor_scalar, qbf[pr][:, :], ln32[pr][:, :], RND_C, -RND_C,
                         ALU.add, ALU.add, reads=[ln32_b[pr]], writes=[qbf_b[pr]])
                    for g in range(4):
                        sl2 = (i * 4 + g) % 2
                        for jj in range(4):
                            j = 4 * g + jj
                            p.upd("T", T.matmul,
                                  pt[:, 512 * sl2 + 128 * jj:512 * sl2 + 128 * (jj + 1)],
                                  qbf[pr][:, 128 * j:128 * (j + 1)], identb[:, :],
                                  start=True, stop=True,
                                  reads=[qbf_b[pr], identb_b], writes=[ptb2[sl2]])
                        p.upd("S", S.activation, qfm[pr][:, 512 * g:512 * (g + 1)],
                              pt[:, 512 * sl2:512 * (sl2 + 1)],
                              AF.Identity, reads=[ptb2[sl2]], writes=[qfm_b[pr]],
                              extra=(p._deps([], [qfm_b[pr]]) if g == 0 else ()))
                    for j in range(ND):
                        for sl in range(4):
                            p.op("T", T.matmul, po[:, 512 * sl:512 * (sl + 1)],
                                 qfm[pr][:, 128 * j:128 * (j + 1)],
                                 fm[:, TOK * j + 512 * sl:TOK * j + 512 * (sl + 1)],
                                 start=(j == 0), stop=(j == ND - 1),
                                 reads=[qfm_b[pr], fmb[j]], writes=[po_b[sl]])
                    for sl in range(4):
                        sls = slice(512 * sl, 512 * (sl + 1))
                        p.op("V", V.tensor_mul, ef32[pr][:, :], po[:, sls], bcs[0][:, sls],
                             reads=[po_b[sl], bcs_b[0]], writes=[ef32_b[pr]])
                        p.op("V", V.scalar_tensor_tensor, ef16[pr][:, :], ef32[pr][:, :],
                             smq[pr][:, 1:2], bcs[1][:, sls], ALU.mult, ALU.add,
                             reads=[ef32_b[pr], smq_b[pr], bcs_b[1]], writes=[ef16_b[pr]])
                        p.op("V", V.scalar_tensor_tensor, xa[u][:, sls], ef16[pr][:, :], 0.5,
                             xa[u][:, sls], ALU.mult, ALU.add,
                             reads=[ef16_b[pr]], writes=[xa_b[u]])
                    h = p.dma("G", dst_ap(s, i), xa[u][:, :], reads=[xa_b[u]], writes=[xres_b[i]])
                    if s == n_sub - 1:
                        st_handles.append(h)
            p.barrier()

        def build_lnfm(s):
            import os
            LNF = os.environ.get("LNF", "c")
            with ExitStack() as E:
                xa = [E.enter_context(nc.sbuf_tensor(f"lf_xa{u}_{s}", [128, 2048], F32))
                      for u in range(2)]
                xa_b = [Buf(), Buf()]
                lnb = [E.enter_context(nc.sbuf_tensor(f"lf_ln{u}_{s}", [128, 2048], BF16))
                       for u in range(2)]
                lnb_b = [Buf(), Buf()]
                sq32 = E.enter_context(nc.sbuf_tensor(f"lf_sq_{s}", [128, 2048], F32))
                sq32_b = Buf()
                pt = E.enter_context(nc.psum_tensor(f"lf_pt_{s}", [128, 2048], F32))
                pt_b = [Buf() for _ in range(4)]
                for i in range(NT):
                    u = i % 2
                    p.dma("G", xa[u][:, :], src_ap(s, i), reads=[xres_b[i]], writes=[xa_b[u]])
                    stats_tile(i, xa[u][:, :], xa_b[u], sq32[:, :], sq32_b)
                    ln_apply(i, lnb[u][:, :], lnb_b[u], xa[u][:, :], xa_b[u])
                    if LNF == "a":
                        continue
                    for j in range(ND):
                        p.op("T", T.matmul, pt[:, 512 * (j % 4):512 * (j % 4) + 128],
                             lnb[u][:, 128 * j:128 * (j + 1)], identb[:, :],
                             start=True, stop=True,
                             reads=[lnb_b[u], identb_b], writes=[pt_b[j % 4]])
                        if LNF == "b":
                            continue
                        p.upd("S", S.activation,
                              fm[:, TOK * j + 128 * i:TOK * j + 128 * (i + 1)],
                              pt[:, 512 * (j % 4):512 * (j % 4) + 128], AF.Identity,
                              reads=[pt_b[j % 4]], writes=[fmb[j]])
            p.barrier()

        def build_mamba(s):
            import os
            MMPH = os.environ.get("MMPH", "123")
            MM2S = os.environ.get("MM2S", "d")
            build_lnfm(s)
            if "0" == MMPH:
                for i in range(NT):
                    xa0 = wsl32 if False else None
                with ExitStack() as E0:
                    xa = E0.enter_context(nc.sbuf_tensor(f"mm_pass_{s}", [128, 2048], F32))
                    xa_b = Buf()
                    for i in range(NT):
                        p.dma("G", xa[:, :], src_ap(s, i), reads=[xres_b[i]], writes=[xa_b])
                        h = p.dma("G", dst_ap(s, i), xa[:, :], reads=[xa_b], writes=[xres_b[i]])
                        if s == n_sub - 1:
                            st_handles.append(h)
                return
            with ExitStack() as E:
                def lsb(name, shape, dt):
                    return E.enter_context(nc.sbuf_tensor(name, shape, dt))
                wsl = [lsb(f"mm_w{u}", [128, 2048], F32) for u in range(2)]
                wsl_b = [Buf(), Buf()]
                wbb = [lsb(f"mm_wb{u}", [128, 2048], BF16) for u in range(2)]
                wbb_b = [Buf(), Buf()]
                acc = [lsb(f"mm_acc{u}", [128, 2048], F32) for u in range(2)]
                acc_b = [Buf(), Buf()]
                ssc = [lsb(f"mm_ssc{u}", [128, 2048], F32) for u in range(2)]
                ssc_b = [Buf(), Buf()]
                xch = [lsb(f"mm_x{u}", [128, 2048], F32) for u in range(2)]
                xch_b = [Buf(), Buf()]
                tnh = [lsb(f"mm_t{u}", [128, 2048], F32) for u in range(2)]
                tnh_b = [Buf(), Buf()]
                dtraw = lsb("mm_dtraw", [64, 2048], F32)
                dtraw_b = Buf()
                dttmp = lsb("mm_dttmp", [64, 2048], F32)
                dttmp_b = Buf()
                dtfin = lsb("mm_dtfin", [64, 2048], F32)
                dtfin_b = Buf()
                dt2 = [lsb(f"mm_dt2_{u}", [2, 2048], F32) for u in range(2)]
                dt2_b = [Buf(), Buf()]
                dtraw_rows = [Buf() for _ in range(NCH)]

                xscr_b = [Buf() for _ in range(NCH)]
                with (
                    nc.psum_tensor("mm_pp", [128, 2048], F32) as pp,
                    nc.psum_tensor("mm_pdt", [2, 2048], F32) as pdt,
                ):
                    pp_b = Buf()
                    pdt_b = Buf()
                    for k in range(NCH):
                        u = k % 2
                        p.dma("SY", wsl[u][:, :], wint[128 * k:128 * (k + 1), :],
                              writes=[wsl_b[u]])
                        p.op("V", V.tensor_copy, wbb[u][:, :], wsl[u][:, :],
                             reads=[wsl_b[u]], writes=[wbb_b[u]])
                        for j in range(ND):
                            for sl in range(4):
                                p.op("T", T.matmul, pp[:, 512 * sl:512 * (sl + 1)],
                                     wbb[u][:, 128 * j:128 * (j + 1)],
                                     fm[:, TOK * j + 512 * sl:TOK * j + 512 * (sl + 1)],
                                     start=(j == 0), stop=(j == ND - 1),
                                     reads=[wbb_b[u], fmb[j]], writes=[pp_b])
                        p.op("S", S.activation, acc[u][:, :], pp[:, :], AF.Identity,
                             scale=consts[:, 4 * k + 3:4 * k + 4],
                             bias=consts[:, 4 * NCH + k:4 * NCH + k + 1],
                             reads=[pp_b, consts_b], writes=[acc_b[u]])
                        for tap in (1, 2, 3):
                            p.op("V", V.scalar_tensor_tensor, acc[u][:, tap:2048],
                                 pp[:, 0:2048 - tap],
                                 consts[:, 4 * k + 3 - tap:4 * k + 4 - tap],
                                 acc[u][:, tap:2048], ALU.mult, ALU.add,
                                 reads=[pp_b, consts_b, acc_b[u]], writes=[acc_b[u]])
                        p.op("S", S.activation, tnh[u][:, :], acc[u][:, :], AF.Tanh, scale=0.5,
                             reads=[acc_b[u]], writes=[tnh_b[u]])
                        p.op("V", V.tensor_scalar, tnh[u][:, :], tnh[u][:, :], 0.5, 0.5,
                             ALU.mult, ALU.add, reads=[tnh_b[u]], writes=[tnh_b[u]])
                        p.op("V", V.tensor_mul, xch[u][:, :], acc[u][:, :], tnh[u][:, :],
                             reads=[acc_b[u], tnh_b[u]], writes=[xch_b[u]])
                        for sl in range(4):
                            p.op("T", T.matmul, pdt[:, 512 * sl:512 * (sl + 1)],
                                 dtwsel[:, 2 * k:2 * k + 2],
                                 xch[u][:, 512 * sl:512 * (sl + 1)], start=True, stop=True,
                                 reads=[dtwsel_b, xch_b[u]], writes=[pdt_b])
                        p.op("S", S.activation, dt2[u][:, :], pdt[:, :],
                             AF.Identity, reads=[pdt_b], writes=[dt2_b[u]])
                        p.dma("G", dtraw[2 * k:2 * k + 2, 0:2048], dt2[u][:, :],
                              reads=[dt2_b[u]], writes=[dtraw_rows[k]])
                        p.dma("G", xscr[128 * k:128 * (k + 1), :], xch[u][:, :],
                              reads=[xch_b[u]], writes=[xscr_b[k]])
                p.barrier()
                p.op("S", S.activation, dtfin[:, :], dtraw[:, :], AF.Identity,
                     bias=dtbs[:, 0:1], reads=[dtraw_b, dtbs_b] + dtraw_rows,
                     writes=[dtfin_b])
                p.op("S", S.activation, dttmp[:, :], dtfin[:, :], AF.Abs,
                     reads=[dtfin_b], writes=[dttmp_b])
                p.op("S", S.activation, dttmp[:, :], dttmp[:, :], AF.Exp, scale=-1.0,
                     reads=[dttmp_b], writes=[dttmp_b])
                p.op("S", S.activation, dttmp[:, :], dttmp[:, :], AF.Ln, bias=1.0,
                     reads=[dttmp_b], writes=[dttmp_b])
                p.op("V", V.tensor_relu, dtraw[:, :], dtfin[:, :],
                     reads=[dtfin_b], writes=[dtraw_b])
                p.op("V", V.tensor_add, dtfin[:, :], dttmp[:, :], dtraw[:, :],
                     reads=[dttmp_b, dtraw_b], writes=[dtfin_b])
                if "2" not in MMPH:
                    with ExitStack() as E0:
                        xa = E0.enter_context(nc.sbuf_tensor(f"mm_pass2_{s}", [128, 2048], F32))
                        xa_b = Buf()
                        for i in range(NT):
                            p.dma("G", xa[:, :], src_ap(s, i), reads=[xres_b[i]], writes=[xa_b])
                            h = p.dma("G", dst_ap(s, i), xa[:, :], reads=[xa_b], writes=[xres_b[i]])
                            if s == n_sub - 1:
                                st_handles.append(h)
                    return
                comb_b = [Buf() for _ in range(NCH)]
                with (
                    nc.psum_tensor("mm_pg", [128, 2048], F32) as pg,
                    nc.psum_tensor("mm_pb", [128, 2048], F32) as pb,
                ):
                    pg_b = Buf()
                    pb_b = Buf()
                    for k in range(NCH):
                        u = k % 2
                        p.dma("G", dt2[u][:, :], dtfin[2 * k:2 * k + 2, :],
                              reads=[dtfin_b], writes=[dt2_b[u]])
                        p.dma("SY", wsl[u][:, :], wgt[128 * k:128 * (k + 1), :],
                              writes=[wsl_b[u]])
                        p.op("V", V.tensor_copy, wbb[u][:, :], wsl[u][:, :],
                             reads=[wsl_b[u]], writes=[wbb_b[u]])
                        p.dma("G", xch[u][:, :], xscr[128 * k:128 * (k + 1), :],
                              reads=[xscr_b[k]], writes=[xch_b[u]])
                        for j in range(ND):
                            for sl in range(4):
                                p.op("T", T.matmul, pg[:, 512 * sl:512 * (sl + 1)],
                                     wbb[u][:, 128 * j:128 * (j + 1)],
                                     fm[:, TOK * j + 512 * sl:TOK * j + 512 * (sl + 1)],
                                     start=(j == 0), stop=(j == ND - 1),
                                     reads=[wbb_b[u], fmb[j]], writes=[pg_b])
                        for sl in range(4):
                            p.op("T", T.matmul, pb[:, 512 * sl:512 * (sl + 1)],
                                 osel[:, :], dt2[u][:, 512 * sl:512 * (sl + 1)],
                                 start=True, stop=True,
                                 reads=[osel_b, dt2_b[u]], writes=[pb_b])
                        if MM2S == "a":
                            p.op("S", S.activation, acc[u][:, :], pg[:, :], AF.Identity,
                                 reads=[pg_b], writes=[acc_b[u]])
                            p.op("S", S.activation, acc[u][:, :], pb[:, :], AF.Identity,
                                 reads=[pb_b], writes=[acc_b[u]])
                            continue
                        p.op("S", S.activation, acc[u][:, :], pb[:, :], AF.Exp,
                             scale=mpar[:, k:k + 1], reads=[pb_b, mpar_b], writes=[acc_b[u]])
                        p.op("S", S.activation, ssc[u][:, :], pb[:, :], AF.Identity,
                             reads=[pb_b], writes=[ssc_b[u]])
                        p.op("V", V.scalar_tensor_tensor, tnh[u][:, :], xch[u][:, :],
                             mpar[:, NCH + k:NCH + k + 1], ssc[u][:, :], ALU.mult, ALU.mult,
                             reads=[xch_b[u], mpar_b, ssc_b[u]], writes=[tnh_b[u]])
                        if MM2S == "b":
                            continue
                        p.op("V", V.tensor_tensor_scan, ssc[u][:, :], acc[u][:, :],
                             tnh[u][:, :], 0.0, ALU.mult, ALU.add,
                             reads=[acc_b[u], tnh_b[u]], writes=[ssc_b[u]])
                        p.op("V", V.scalar_tensor_tensor, tnh[u][:, :], xch[u][:, :],
                             mpar[:, 2 * NCH + k:2 * NCH + k + 1], ssc[u][:, :],
                             ALU.mult, ALU.add,
                             reads=[xch_b[u], mpar_b, ssc_b[u]], writes=[tnh_b[u]])
                        if MM2S == "c":
                            continue
                        p.op("S", S.activation, acc[u][:, :], pg[:, :], AF.Tanh, scale=0.5,
                             reads=[pg_b], writes=[acc_b[u]])
                        p.op("V", V.tensor_scalar, acc[u][:, :], acc[u][:, :], 0.5, None,
                             ALU.mult, reads=[acc_b[u]], writes=[acc_b[u]])
                        p.op("V", V.tensor_scalar, acc[u][:, :], acc[u][:, :], 0.5, None,
                             ALU.add, reads=[acc_b[u]], writes=[acc_b[u]])
                        p.op("V", V.tensor_mul, wbb[u][:, :], tnh[u][:, :], acc[u][:, :],
                             reads=[tnh_b[u], acc_b[u]], writes=[wbb_b[u]])
                        p.dma("G", combscr[128 * k:128 * (k + 1), :], wbb[u][:, :],
                              reads=[wbb_b[u]], writes=[comb_b[k]])
                if "3" not in MMPH:
                    with ExitStack() as E0:
                        xa = E0.enter_context(nc.sbuf_tensor(f"mm_pass3_{s}", [128, 2048], F32))
                        xa_b = Buf()
                        for i in range(NT):
                            p.dma("G", xa[:, :], src_ap(s, i), reads=[xres_b[i]], writes=[xa_b])
                            h = p.dma("G", dst_ap(s, i), xa[:, :], reads=[xa_b], writes=[xres_b[i]])
                            if s == n_sub - 1:
                                st_handles.append(h)
                    return
                cstrip = [lsb(f"mm_cs{u}", [128, 2048], BF16) for u in range(2)]
                cs_b = [Buf(), Buf()]
                with nc.psum_tensor("mm_po", [128, 2048], F32) as po:
                    po_b = [Buf() for _ in range(4)]
                    for half in range(2):
                        for j in range(ND):
                            kk = 16 * half + j
                            p.dma("SY", wsl[j % 2][:, :], wot[128 * kk:128 * (kk + 1), :],
                                  writes=[wsl_b[j % 2]])
                            p.op("V", V.tensor_copy, fm[:, TOK * j:TOK * (j + 1)],
                                 wsl[j % 2][:, :], reads=[wsl_b[j % 2]], writes=[fmb[j]])
                        for i in range(NT):
                            u = i % 2
                            src = combscr[2048 * half:2048 * (half + 1),
                                          128 * i:128 * (i + 1)]
                            p.dma("G", cstrip[u][:, :].rearrange("p (j c) -> p j c", c=128),
                                  src.rearrange("(j p) c -> p j c", p=128),
                                  reads=[comb_b[16 * half + jj] for jj in range(ND)],
                                  writes=[cs_b[u]])
                            for j in range(ND):
                                for sl in range(4):
                                    p.op("T", T.matmul, po[:, 512 * sl:512 * (sl + 1)],
                                         cstrip[u][:, 128 * j:128 * (j + 1)],
                                         fm[:, TOK * j + 512 * sl:TOK * j + 512 * (sl + 1)],
                                         start=(j == 0), stop=(j == ND - 1),
                                         reads=[cs_b[u], fmb[j]], writes=[po_b[sl]])
                            if half == 0:
                                p.op("S", S.activation, acc[i % 2][:, :], po[:, :], AF.Identity,
                                     reads=[po_b[0], po_b[1], po_b[2], po_b[3]],
                                     writes=[acc_b[i % 2]])
                                p.dma("G", xscr[128 * i:128 * (i + 1), :], acc[i % 2][:, :],
                                      reads=[acc_b[i % 2]], writes=[xscr_b[i]])
                            else:
                                u = i % 2
                                p.dma("G", xch[u][:, :], xscr[128 * i:128 * (i + 1), :],
                                      reads=[xscr_b[i]], writes=[xch_b[u]])
                                p.dma("SY", acc[u][:, :], src_ap(s, i),
                                      reads=[xres_b[i]], writes=[acc_b[u]])
                                p.op("V", V.tensor_add, xch[u][:, :], xch[u][:, :], po[:, :],
                                     reads=[xch_b[u], po_b[0], po_b[1], po_b[2], po_b[3]],
                                     writes=[xch_b[u]])
                                p.op("V", V.tensor_add, xch[u][:, :], xch[u][:, :],
                                     acc[u][:, :], reads=[xch_b[u], acc_b[u]],
                                     writes=[xch_b[u]])
                                h = p.dma("G", dst_ap(s, i), xch[u][:, :],
                                          reads=[xch_b[u]], writes=[xres_b[i]])
                                if s == n_sub - 1:
                                    st_handles.append(h)
            p.barrier()

        def build_ffn1(s, callno):
            with ExitStack() as E:
                def lsb(name, shape, dt):
                    return E.enter_context(nc.sbuf_tensor(name, shape, dt))
                xa = [lsb(f"f1_xa{u}_{callno}", [128, 2048], F32) for u in range(2)]
                xa_b = [Buf(), Buf()]
                ln32 = lsb(f"f1_ln_{callno}", [128, 2048], F32)
                ln32_b = Buf()
                qbf = lsb(f"f1_qbf_{callno}", [128, 2048], BF16)
                qbf_b = Buf()
                qfm = lsb(f"f1_qfm_{callno}", [128, 2048], BF16)
                qfm_b = Buf()
                f8s = lsb(f"f1_f8_{callno}", [128, 2048], F8)
                f8s_b = Buf()
                bcb = lsb(f"f1_bcb_{callno}", [128, 2048], BF16)
                bcb_b = Buf()
                ebf = lsb(f"f1_ebf_{callno}", [128, 512], BF16)
                ebf_b = Buf()
                pt = E.enter_context(nc.psum_tensor(f"f1_pt_{callno}", [128, 512], F32))
                pt_b = [Buf() for _ in range(4)]
                po = E.enter_context(nc.psum_tensor(f"f1_po_{callno}", [128, 2048], F32))
                po_b = [Buf() for _ in range(4)]
                psc = E.enter_context(nc.psum_tensor(f"f1_psc_{callno}", [128, 128], F32))
                psc_b = Buf()
                p.dma("SY", bcb[:, :], b1_bc, writes=[bcb_b])
                for j in range(ND):
                    p.dma("SY", fm[:, TOK * j:TOK * (j + 1)], w1t[128 * j:128 * (j + 1), :],
                          writes=[fmb[j]])
                    p.upd("V", V.reduce_max, smalls[:, 8 + j:9 + j],
                          fm[:, TOK * j:TOK * (j + 1)], AX.X, True,
                          reads=[fmb[j]], writes=[smalls_b])
                p.op("V", V.reduce_max, smalls[:, 24:25], smalls[:, 8:24], AX.X,
                     reads=[smalls_b], writes=[smalls_b])
                colmax_to_row1(smalls[:, 24:25], smalls_b, psc, psc_b)
                p.upd("V", V.tensor_scalar_max, row1[0:1, 0:1], row1[0:1, 0:1], 1e-6,
                      reads=[row1_b], writes=[row1_b])
                p.upd("V", V.tensor_copy, row1[0:1, 1:2], row1[0:1, 0:1],
                      reads=[row1_b], writes=[row1_b])
                p.upd("V", V.reciprocal, row1[0:1, 2:3], row1[0:1, 1:2],
                      reads=[row1_b], writes=[row1_b])
                bcast_scalar(row1[0:1, 2:3], row1_b, 0, psc, psc_b)
                for j in range(ND):
                    p.op("V", V.tensor_scalar, f8s[:, :], fm[:, TOK * j:TOK * (j + 1)],
                         xsws[:, 0:1], None, ALU.mult,
                         reads=[fmb[j], xsws_b], writes=[f8s_b])
                    p.op("V", V.tensor_copy, fm[:, TOK * j:TOK * (j + 1)], f8s[:, :],
                         reads=[f8s_b], writes=[fmb[j]])
                for i in range(NT):
                    u = i % 2
                    p.dma("G", xa[u][:, :], src_ap(s, i), reads=[xres_b[i]], writes=[xa_b[u]])
                    stats_tile(i, xa[u][:, :], xa_b[u], ln32[:, :], ln32_b)
                    ln_apply(i, ln32[:, :], ln32_b, xa[u][:, :], xa_b[u])
                    p.upd("V", V.reduce_max, smalls[:, 8 + i:9 + i], ln32[:, :], AX.X, True,
                          reads=[ln32_b], writes=[smalls_b])
                p.op("V", V.reduce_max, smalls[:, 24:25], smalls[:, 8:24], AX.X,
                     reads=[smalls_b], writes=[smalls_b])
                colmax_to_row1(smalls[:, 24:25], smalls_b, psc, psc_b)
                p.upd("V", V.tensor_scalar_max, row1[0:1, 0:1], row1[0:1, 0:1], 1e-6,
                      reads=[row1_b], writes=[row1_b])
                p.upd("V", V.memset, row1[0:1, 3:7], 0.0, reads=[row1_b], writes=[row1_b])
                p.upd("V", V.tensor_copy, row1[0:1, 3:4], row1[0:1, 0:1],
                      reads=[row1_b], writes=[row1_b])
                p.dma("SY", cc_in[callno], row1[0:1, 3:7], reads=[row1_b])
                p.op("G", G.collective_compute, "AllReduce", ALU.max,
                     [[0, 1, 2, 3, 4, 5, 6, 7]], [cc_in[callno]], [cc_out[callno]],
                     extra=[Hd(p.sems["SY"], p.cnt["SY"])])
                p.dma("SY", row1[0:1, 3:7], cc_out[callno],
                      extra=[Hd(p.sems["G"], p.cnt["G"])], writes=[row1_b])
                p.upd("V", V.reciprocal, row1[0:1, 4:5], row1[0:1, 3:4],
                      reads=[row1_b], writes=[row1_b])
                p.upd("V", V.tensor_mul, row1[0:1, 5:6], row1[0:1, 3:4], row1[0:1, 1:2],
                      reads=[row1_b], writes=[row1_b])
                bcast_scalar(row1[0:1, 4:5], row1_b, 1, psc, psc_b)
                bcast_scalar(row1[0:1, 5:6], row1_b, 2, psc, psc_b)
                if DBG and callno == 0:
                    p.dma("G", dbg["d_row"], row1[:, :], reads=[row1_b])
                    p.dma("G", dbg["d_xsws"], xsws[:, :], reads=[xsws_b])
                    p.dma("G", dbg["d_cc"], cc_in[callno], extra=[Hd(p.sems["G"], p.cnt["G"])])
                for i in range(NT):
                    u = i % 2
                    p.dma("G", xa[u][:, :], src_ap(s, i), reads=[xres_b[i]], writes=[xa_b[u]])
                    ln_apply(i, ln32[:, :], ln32_b, xa[u][:, :], xa_b[u])
                    p.op("V", V.tensor_scalar, f8s[:, :], ln32[:, :], xsws[:, 1:2], None,
                         ALU.mult, reads=[ln32_b, xsws_b], writes=[f8s_b])
                    p.op("V", V.tensor_copy, qbf[:, :], f8s[:, :],
                         reads=[f8s_b], writes=[qbf_b])
                    for g in range(4):
                        for jj in range(4):
                            j = 4 * g + jj
                            p.op("T", T.matmul, pt[:, 128 * jj:128 * (jj + 1)],
                                 qbf[:, 128 * j:128 * (j + 1)], identb[:, :],
                                 start=True, stop=True,
                                 reads=[qbf_b, identb_b], writes=[pt_b[jj]])
                        p.op("S", S.activation, qfm[:, 512 * g:512 * (g + 1)], pt[:, :],
                             AF.Identity, reads=[pt_b[0], pt_b[1], pt_b[2], pt_b[3]],
                             writes=[qfm_b])
                    for j in range(ND):
                        for sl in range(4):
                            p.op("T", T.matmul, po[:, 512 * sl:512 * (sl + 1)],
                                 qfm[:, 128 * j:128 * (j + 1)],
                                 fm[:, TOK * j + 512 * sl:TOK * j + 512 * (sl + 1)],
                                 start=(j == 0), stop=(j == ND - 1),
                                 reads=[qfm_b, fmb[j]], writes=[po_b[sl]])
                    if DBG and callno == 0 and i == 0:
                        p.dma("G", dbg["d_q8"], qbf[:, :], reads=[qbf_b])
                        p.dma("G", dbg["d_w8"], fm[:, 0:TOK], reads=[fmb[0]])
                        dpo2 = nc.sbuf_tensor(f"d_po2_sb_{callno}", [128, 2048], F32)
                        with dpo2 as dpo2_t:
                            dpo2_b = Buf()
                            p.op("S", S.activation, dpo2_t[:, :], po[:, :], AF.Identity,
                                 reads=[po_b[0], po_b[1], po_b[2], po_b[3]], writes=[dpo2_b])
                            p.dma("G", dbg["d_po1"], dpo2_t[:, :], reads=[dpo2_b])
                    for sl in range(4):
                        sls = slice(512 * sl, 512 * (sl + 1))
                        p.op("S", S.activation, ebf[:, :], po[:, sls], AF.Identity,
                             scale=xsws[:, 2:3], reads=[po_b[sl], xsws_b], writes=[ebf_b])
                        p.op("V", V.tensor_add, ebf[:, :], ebf[:, :], bcb[:, sls],
                             reads=[ebf_b, bcb_b], writes=[ebf_b])
                        p.op("V", V.scalar_tensor_tensor, xa[u][:, sls], ebf[:, :], 0.5,
                             xa[u][:, sls], ALU.mult, ALU.add,
                             reads=[ebf_b], writes=[xa_b[u]])
                    h = p.dma("G", dst_ap(s, i), xa[u][:, :], reads=[xa_b[u]], writes=[xres_b[i]])
                    if s == n_sub - 1:
                        st_handles.append(h)
            p.barrier()

        def build_attn(s):
            build_lnfm(s)
            with ExitStack() as E:
                def lsb(name, shape, dt):
                    return E.enter_context(nc.sbuf_tensor(name, shape, dt))
                qfm = lsb("at_qfm", [128, 16 * 2048], BF16)
                qfm_b = [Buf() for _ in range(16)]
                kfm = lsb("at_kfm", [128, 4 * 2048], BF16)
                kfm_b = [Buf() for _ in range(4)]
                vfm = lsb("at_vfm", [128, 4 * 2048], BF16)
                vfm_b = [Buf() for _ in range(4)]
                E2 = E.enter_context(ExitStack())
                def lsb2(name, shape, dt):
                    return E2.enter_context(nc.sbuf_tensor(name, shape, dt))
                wsl = [lsb2(f"at_w{u}", [128, 2048], F32) for u in range(2)]
                wsl_b = [Buf(), Buf()]
                enb = lsb2("at_enb0", [128, 2048], BF16)
                enb_b = Buf()
                with nc.psum_tensor("at_pq", [128, 2048], F32) as pq:
                    pq_b = [Buf() for _ in range(4)]
                    for (wt, nchunk, dstarena, dst_b, btcol) in (
                        (wq_t, 16, qfm, qfm_b, 0), (wk_t, 4, kfm, kfm_b, 16),
                        (wv_t, 4, vfm, vfm_b, 20),
                    ):
                        for cch in range(nchunk):
                            u = cch % 2
                            p.dma("SY", wsl[u][:, :], wt[128 * cch:128 * (cch + 1), :],
                                  writes=[wsl_b[u]])
                            p.op("V", V.tensor_copy, enb[:, :], wsl[u][:, :],
                                 reads=[wsl_b[u]], writes=[enb_b])
                            for j in range(ND):
                                for sl in range(4):
                                    p.op("T", T.matmul, pq[:, 512 * sl:512 * (sl + 1)],
                                         enb[:, 128 * j:128 * (j + 1)],
                                         fm[:, TOK * j + 512 * sl:TOK * j + 512 * (sl + 1)],
                                         start=(j == 0), stop=(j == ND - 1),
                                         reads=[enb_b, fmb[j]], writes=[pq_b[sl]])
                            p.op("S", S.activation, dstarena[:, 2048 * cch:2048 * (cch + 1)],
                                 pq[:, :], AF.Identity,
                                 bias=biasq[:, btcol + cch:btcol + cch + 1],
                                 reads=[pq_b[0], pq_b[1], pq_b[2], pq_b[3], biasq_b],
                                 writes=[dst_b[cch]])
                p.barrier()
                E2.close()
                E3 = E.enter_context(ExitStack())
                def lsb3(name, shape, dt):
                    return E3.enter_context(nc.sbuf_tensor(name, shape, dt))
                krep = lsb3("at_krep", [128, 2048], BF16)
                krep_b = Buf()
                vrep = lsb3("at_vrep", [128, 2048], BF16)
                vrep_b = Buf()
                enb2 = [lsb3(f"at_enbp{u}", [128, 2048], BF16) for u in range(2)]
                enb2_b = [Buf(), Buf()]
                sm2 = [lsb3(f"at_sm{u}", [128, 2], F32) for u in range(2)]
                sm2_b = [Buf(), Buf()]
                ent = lsb3("at_ent", [128, 1024], BF16)
                entb = [Buf(), Buf()]
                pttb = [Buf(), Buf()]
                with (
                    nc.psum_tensor("at_ps", [128, 2048], F32) as ps,
                    nc.psum_tensor("at_pt", [128, 1024], F32) as ptt,
                    nc.psum_tensor("at_po", [128, 1024], F32) as poh,
                ):
                    ps_b = Buf()
                    ptt_b = Buf()
                    poh_b = Buf()
                    for h in range(16):
                        g = h // 4
                        r = h % 4
                        for sl in range(4):
                            p.op("T", T.matmul, ptt[:, 0:512],
                                 sel128s[:, 128 * r:128 * (r + 1)],
                                 kfm[:, 2048 * g + 512 * sl:2048 * g + 512 * (sl + 1)],
                                 start=True, stop=True,
                                 reads=[sel128s_b, kfm_b[g]],
                                 writes=[ptt_b],
                                 extra=(p._deps([pttb[0], pttb[1]], []) if sl == 0 else ()))
                            p.upd("S", S.activation, krep[:, 512 * sl:512 * (sl + 1)],
                                  ptt[:, 0:512], AF.Identity, reads=[ptt_b], writes=[krep_b])
                        for gq in range(4):
                            for jj in range(4):
                                tkc = 4 * gq + jj
                                p.op("T", T.matmul, ptt[:, 128 * jj:128 * (jj + 1)],
                                     vfm[:, 2048 * g + 128 * tkc:2048 * g + 128 * (tkc + 1)],
                                     sel128s[:, 128 * r:128 * (r + 1)],
                                     start=True, stop=True,
                                     reads=[vfm_b[g], sel128s_b], writes=[ptt_b])
                            p.upd("S", S.activation, vrep[:, 512 * gq:512 * (gq + 1)],
                                  ptt[:, 0:512], AF.Identity, reads=[ptt_b],
                                  writes=[vrep_b] if gq == 3 else [],
                                  extra=p._deps([], [vrep_b]) if gq == 0 else ())
                        for half in range(2):
                            for i in range(8 * half, 8 * half + 8):
                                ncol = (i + 1) * 128
                                nsl = (ncol + 511) // 512
                                pr = i % 2
                                for sl in range(nsl):
                                    w = min(512, ncol - 512 * sl)
                                    p.op("T", T.matmul, ps[:, 512 * sl:512 * sl + w],
                                         qfm[:, 2048 * h + 128 * i:2048 * h + 128 * (i + 1)],
                                         krep[:, 512 * sl:512 * sl + w],
                                         start=True, stop=True,
                                         reads=[qfm_b[h], krep_b], writes=[ps_b])
                                p.op("V", V.tensor_add, ps[:, 128 * i:128 * (i + 1)],
                                     ps[:, 128 * i:128 * (i + 1)], masks[:, :],
                                     reads=[ps_b, masks_b], writes=[ps_b],
                                     extra=p._deps([ps_b], []))
                                h_exp = p.op("S", S.activation, enb2[pr][:, 0:ncol],
                                             ps[:, 0:ncol], AF.Exp, scale=SM_SCALE,
                                             accum_out=sm2[pr][:, 0:1],
                                             reads=[ps_b], writes=[enb2_b[pr]],
                                             extra=p._deps([], [sm2_b[pr]]))
                                sm2_b[pr].w = [h_exp]
                                sm2_b[pr].r = []
                                p.upd("V", V.reciprocal, sm2[pr][:, 1:2], sm2[pr][:, 0:1],
                                      reads=[sm2_b[pr]], writes=[sm2_b[pr]])
                                p.op("V", V.tensor_scalar, enb2[pr][:, 0:ncol],
                                     enb2[pr][:, 0:ncol], sm2[pr][:, 1:2], None, ALU.mult,
                                     reads=[enb2_b[pr], sm2_b[pr]], writes=[enb2_b[pr]])
                                for gq in range((i + 4) // 4):
                                    ue = gq % 2
                                    t0 = 4 * gq
                                    nt = min(4, i + 1 - t0)
                                    for tt in range(nt):
                                        p.upd("T", T.matmul,
                                              ptt[:, 512 * ue + 128 * tt:512 * ue + 128 * (tt + 1)],
                                              enb2[pr][:, 128 * (t0 + tt):128 * (t0 + tt + 1)],
                                              identb[:, :], start=True, stop=True,
                                              reads=[enb2_b[pr], identb_b], writes=[pttb[ue]])
                                    p.op("S", S.activation, ent[:, 512 * ue:512 * ue + 128 * nt],
                                         ptt[:, 512 * ue:512 * ue + 128 * nt], AF.Identity,
                                         reads=[pttb[ue]], writes=[entb[ue]])
                                    for tt in range(nt):
                                        tkc = t0 + tt
                                        p.op("T", T.matmul,
                                             poh[:, 128 * (i - 8 * half):128 * (i - 8 * half + 1)],
                                             vrep[:, 128 * tkc:128 * (tkc + 1)],
                                             ent[:, 512 * ue + 128 * tt:512 * ue + 128 * (tt + 1)],
                                             start=(tkc == 0), stop=(tkc == i),
                                             reads=[vrep_b, entb[ue]], writes=[poh_b])
                            p.upd("S", S.activation,
                                  fm[:, TOK * h + 1024 * half:TOK * h + 1024 * (half + 1)],
                                  poh[:, :], AF.Identity, reads=[poh_b], writes=[fmb[h]])
                p.barrier()
                E3.close()
                E4 = E.enter_context(ExitStack())
                def lsb4(name, shape, dt):
                    return E4.enter_context(nc.sbuf_tensor(name, shape, dt))
                wsl = [lsb4(f"at_ww{u}", [128, 2048], F32) for u in range(2)]
                wsl_b = [Buf(), Buf()]
                xa = [lsb4(f"at_xa{u}", [128, 2048], F32) for u in range(2)]
                xa_b = [Buf(), Buf()]
                bcs = lsb4("at_bc", [128, 2048], F32)
                bcs_b = Buf()
                p.dma("SY", bcs[:, :], bo_bc, writes=[bcs_b])
                for rc in range(16):
                    u = rc % 2
                    p.dma("SY", wsl[u][:, :], wo[128 * rc:128 * (rc + 1), :],
                          writes=[wsl_b[u]])
                    p.op("V", V.tensor_copy, qfm[:, 2048 * rc:2048 * (rc + 1)], wsl[u][:, :],
                         reads=[wsl_b[u]], writes=[qfm_b[rc]])
                with nc.psum_tensor("at_pw", [128, 2048], F32) as pw:
                    pw_b = [Buf() for _ in range(4)]
                    for i in range(NT):
                        u = i % 2
                        p.dma("G", xa[u][:, :], src_ap(s, i), reads=[xres_b[i]],
                              writes=[xa_b[u]])
                        for rc in range(16):
                            for sl in range(4):
                                p.op("T", T.matmul, pw[:, 512 * sl:512 * (sl + 1)],
                                     fm[:, TOK * rc + 128 * i:TOK * rc + 128 * (i + 1)],
                                     qfm[:, 2048 * rc + 512 * sl:2048 * rc + 512 * (sl + 1)],
                                     start=(rc == 0), stop=(rc == 15),
                                     reads=[fmb[rc], qfm_b[rc]], writes=[pw_b[sl]])
                        for sl in range(4):
                            sls = slice(512 * sl, 512 * (sl + 1))
                            p.upd("V", V.tensor_add, xa[u][:, sls], xa[u][:, sls], pw[:, sls],
                                  reads=[pw_b[sl], xa_b[u]], writes=[xa_b[u]])
                            p.upd("V", V.tensor_add, xa[u][:, sls], xa[u][:, sls], bcs[:, sls],
                                  reads=[bcs_b, xa_b[u]], writes=[xa_b[u]])
                        h2 = p.dma("G", dst_ap(s, i), xa[u][:, :], reads=[xa_b[u]],
                                   writes=[xres_b[i]])
                        if s == n_sub - 1:
                            st_handles.append(h2)
            p.barrier()

        builders = [
            lambda s: build_ffn0(s),
            lambda s: build_mamba(s),
            lambda s: build_ffn0(s),
            lambda s: build_ffn1(s, 0),
            lambda s: build_attn(s),
            lambda s: build_ffn1(s, 1),
        ]
        for sidx in range(n_sub):
            builders[sidx](sidx)

        p.finish(st_handles)
    p.close()
    return nc


def _host_layout(inputs):
    f32 = np.float32
    g = lambda k: np.asarray(inputs[k])
    x = g("L0_attn_hidden_states")
    w0 = g("L0_ffn_weight"); w0sc = g("L0_ffn_w_scale"); b0 = g("L0_ffn_bias")
    win = g("L0_attn_in_proj_w"); wg = g("L0_attn_gate_proj_w")
    wout = g("L0_attn_out_proj_w")
    cw = g("L0_attn_conv_weight"); cb = g("L0_attn_conv_bias")
    dtw = g("L0_attn_dt_w"); dtb = g("L0_attn_dt_b")
    alog = g("L0_attn_A_log"); B = g("L0_attn_B"); C = g("L0_attn_C"); Dp = g("L0_attn_D")
    wq = g("L1_attn_Wq"); bq = g("L1_attn_bq")
    wk = g("L1_attn_Wk"); bk = g("L1_attn_bk")
    wv = g("L1_attn_Wv"); bv = g("L1_attn_bv")
    wo = g("L1_attn_Wo"); bo = g("L1_attn_bo")
    w1 = g("L1_ffn_weight"); b1 = g("L1_ffn_bias")
    ca = np.ascontiguousarray

    def slab_tiles(w_T_like, nchunk):
        d_in, n_out = w_T_like.shape
        nj = d_in // 128
        a = w_T_like.reshape(nj, 128, nchunk, 128)
        a = a.transpose(2, 1, 0, 3).reshape(nchunk * 128, nj * 128)
        return ca(a)

    common = {
        "w0t": ca(w0.T),
        "w0sc_bc": ca(np.broadcast_to(w0sc[:, 0][None, :], (128, D))).astype(f32),
        "b0_bc": ca(np.broadcast_to(b0[None, :], (128, D))).astype(f32),
        "wint": slab_tiles(win.T, NCH),
        "wgt": slab_tiles(wg.T, NCH),
        "wot": ca(wout.T),
        "convw": ca(cw.reshape(NCH, 128, 4).transpose(1, 0, 2).reshape(128, NCH * 4)),
        "convb": ca(cb.reshape(NCH, 128).T),
        "dtb_col": ca(dtb[:, None]).astype(f32),
        "alog_t": ca(alog.reshape(-1).reshape(NCH, 128).T),
        "b_t": ca(B.reshape(-1).reshape(NCH, 128).T),
        "c_t": ca(C.reshape(-1).reshape(NCH, 128).T),
        "d_t": ca(Dp.reshape(-1).reshape(NCH, 128).T),
        "wq_t": slab_tiles(wq, 16),
        "wk_t": slab_tiles(wk, 4),
        "wv_t": slab_tiles(wv, 4),
        "wo": ca(wo),
        "bq_t": ca(bq.reshape(16, 128).T), "bk_t": ca(bk.reshape(4, 128).T),
        "bv_t": ca(bv.reshape(4, 128).T),
        "bo_bc": ca(np.broadcast_to(bo[None, :], (128, D))).astype(f32),
        "w1t": ca(w1.T), "b1_bc": ca(np.broadcast_to(b1[None, :], (128, D))),
    }
    dts = np.zeros((128, 64), f32)
    for k in range(NCH):
        dts[0:64, 2 * k] = dtw[2 * k, :]
        dts[64:128, 2 * k + 1] = dtw[2 * k + 1, :]
    common["dtw_sel"] = dts
    ons = np.zeros((2, 128), f32); ons[0, 0:64] = 1; ons[1, 64:128] = 1
    common["onesel2"] = ons
    common["ones1"] = np.ones((1, 128), f32)
    common["ident_bf"] = np.eye(128).astype(ml_dtypes.bfloat16)
    common["ident_f32"] = np.eye(128, dtype=f32)
    mk = np.zeros((128, 128), f32)
    mk[np.triu_indices(128, 1)] = -1e30
    common["maskadd"] = mk
    s128 = np.zeros((4, 128, 128), f32)
    for r in range(4):
        for hd in range(128):
            s128[r, r * 32 + hd // 4, hd] = 1.0
    common["sel128"] = ca(s128.transpose(1, 0, 2).reshape(128, 512)).astype(ml_dtypes.bfloat16)
    maps = []
    for c in range(8):
        m = dict(common)
        m["x"] = ca(x[c])
        maps.append(m)
    return maps


def kernel(**inputs):
    maps = _host_layout(inputs)
    nc = build_nc()
    res = run_bass_kernel_spmd(nc, maps, core_ids=list(range(8)))
    out = np.stack([np.asarray(r["y"]) for r in res.results], axis=0)
    return np.ascontiguousarray(out.astype(np.float32))


# version 11 (final)
# speedup vs baseline: 1.0409x; 1.0409x over previous
import math
import numpy as np
import ml_dtypes
import concourse.bass as bass
import concourse.mybir as mybir
from concourse.bass_utils import run_bass_kernel_spmd

F32 = mybir.dt.float32
BF16 = mybir.dt.bfloat16
F16 = mybir.dt.float16
F8 = mybir.dt.float8e4
I8 = mybir.dt.int8
AF = mybir.ActivationFunctionType
ALU = mybir.AluOpType
AX = mybir.AxisListType

TOK = 2048
D = 2048
INNER = 4096
NT = 16
ND = 16
NCH = 32
RND_C = float(np.float32(1.5 * 2.0 ** 23))
INV127 = float(np.float32(1.0 / 127.0))
SM_SCALE = 1.0 / float(np.float32(math.sqrt(np.float32(128.0))))
N_SUB = 6
DBG = False


class Hd:
    __slots__ = ("sem", "val", "small")
    def __init__(self, sem, val, small=False):
        self.sem, self.val, self.small = sem, val, small


class Buf:
    __slots__ = ("w", "r")
    def __init__(self):
        self.w = []
        self.r = []
    def reset(self):
        self.w = []
        self.r = []


class P:
    def __init__(self, nc):
        self.nc = nc
        self._stack = []
        self.sems = {}
        self.cnt = {}
        self.last_wait = {}
        for name in ("T", "V", "S", "G", "SY", "GD", "SD"):
            self.sems[name] = self._enter(nc.semaphore(f"sem_{name}"))
            self.cnt[name] = 0
        self.eng = {"T": nc.tensor, "V": nc.vector, "S": nc.scalar,
                    "G": nc.gpsimd, "SY": nc.sync}
        self.own = {"T": self.sems["T"], "V": self.sems["V"],
                    "S": self.sems["S"], "G": self.sems["G"]}
        self.dmaq = {"SY": ("SY", "SY"), "G": ("G", "GD"), "S": ("S", "SD")}
        self.last_drain = {"V": 0, "S": 0}
        self.flush_cover = {"V": 0, "S": 0}
        self.flush_val = {"V": 0, "S": 0}

    def _enter(self, cm):
        v = cm.__enter__()
        self._stack.append(cm)
        return v

    def _flush(self, pname):
        cover = self.cnt[pname]
        r = self.eng[pname].drain()
        self.cnt[pname] += 1
        r.then_inc(self.sems[pname], 1)
        self.flush_cover[pname] = cover
        self.flush_val[pname] = self.cnt[pname]
        self.last_drain[pname] = cover

    def _wait(self, ename, deps):
        eng = self.eng[ename]
        own = self.own.get(ename)
        need_drain = False
        for h in deps:
            if h is None:
                continue
            if h.sem is own:
                if ename in ("V", "S") and h.val > self.last_drain[ename]:
                    need_drain = True
                continue
            val = h.val
            if h.small:
                for pn in ("V", "S"):
                    if h.sem is self.sems[pn]:
                        if h.val > self.flush_cover[pn]:
                            self._flush(pn)
                        val = max(val, self.flush_val[pn])
                        break
            key = (ename, h.sem.name)
            if self.last_wait.get(key, -1) >= val:
                continue
            eng.wait_ge(h.sem, val)
            self.last_wait[key] = val
        if need_drain:
            eng.drain()
            self.last_drain[ename] = self.cnt[ename]

    def _deps(self, reads, writes):
        deps = []
        for b in reads:
            deps.extend(b.w)
        for b in writes:
            deps.extend(b.w)
            deps.extend(b.r)
        return deps

    @staticmethod
    def _is_small(args):
        try:
            shp = args[0].shape
            n = 1
            for d in shp:
                n *= int(d)
            return n <= 4096
        except Exception:
            return True

    def op(self, ename, fn, *args, reads=(), writes=(), extra=(), **kw):
        deps = self._deps(reads, writes) + list(extra)
        self._wait(ename, deps)
        r = fn(*args, **kw)
        self.cnt[ename] += 1
        h = Hd(self.sems[ename], self.cnt[ename], self._is_small(args))
        r.then_inc(h.sem, 1)
        for b in reads:
            b.r.append(h)
        for b in writes:
            b.w = [h]
            b.r = []
        return h

    def upd(self, ename, fn, *args, reads=(), writes=(), extra=(), **kw):
        deps = self._deps(reads, ()) + list(extra)
        for b in writes:
            deps.extend(b.r)
        self._wait(ename, deps)
        r = fn(*args, **kw)
        self.cnt[ename] += 1
        h = Hd(self.sems[ename], self.cnt[ename], self._is_small(args))
        r.then_inc(h.sem, 1)
        for b in reads:
            b.r.append(h)
        for b in writes:
            b.w.append(h)
        return h

    def dma(self, q, out_ap, in_ap, reads=(), writes=(), extra=(), accum=False):
        ename, sname = self.dmaq[q]
        deps = self._deps(reads, writes) + list(extra)
        if self.cnt[sname]:
            deps.append(Hd(self.sems[sname], self.cnt[sname]))
        self._wait(ename, deps)
        r = self.eng[ename].dma_start(out_ap, in_ap)
        self.cnt[sname] += 16
        h = Hd(self.sems[sname], self.cnt[sname])
        r.then_inc(h.sem, 16)
        for b in reads:
            b.r.append(h)
        for b in writes:
            b.w = [h]
            b.r = []
        return h

    def barrier(self):
        snap = dict(self.cnt)
        for ename in ("T", "V", "S", "G", "SY"):
            eng = self.eng[ename]
            own = self.own.get(ename)
            for sname, val in snap.items():
                sem = self.sems[sname]
                if sem is own or val == 0:
                    continue
                key = (ename, sem.name)
                if self.last_wait.get(key, -1) >= val:
                    continue
                eng.wait_ge(sem, val)
                self.last_wait[key] = val
        if snap["V"] > self.last_drain["V"]:
            self.eng["V"].drain()
            self.last_drain["V"] = snap["V"]
        if snap["S"] > self.last_drain["S"]:
            self.eng["S"].drain()
            self.last_drain["S"] = snap["S"]

    def finish(self, handles):
        self._wait("SY", handles)

    def close(self):
        for cm in reversed(self._stack):
            cm.__exit__(None, None, None)


def build_nc(n_sub=N_SUB):
    nc = bass.Bass("TRN2", target_bir_lowering=False)
    p = P(nc)
    T, V, S, G = nc.tensor, nc.vector, nc.scalar, nc.gpsimd
    from contextlib import ExitStack

    def din(name, shape, dt):
        return nc.dram_tensor(name, shape, dt, kind="ExternalInput").ap()

    x_in = din("x", [TOK, D], F32)
    y_out = nc.dram_tensor("y", [TOK, D], F32, kind="ExternalOutput").ap()
    xres = nc.dram_tensor("xres", [TOK, D], F32, kind="Internal").ap()
    xscr = nc.dram_tensor("xscr", [INNER, TOK], F32, kind="Internal").ap()
    combscr = nc.dram_tensor("combscr", [INNER, TOK], BF16, kind="Internal").ap()
    cc_in = [nc.dram_tensor(f"cc_in{i}", [1, 4], F32, kind="Internal").ap() for i in range(2)]
    cc_out = [nc.dram_tensor(f"cc_out{i}", [1, 4], F32, kind="Internal",
                             addr_space="Shared").ap() for i in range(2)]

    w0t = din("w0t", [D, D], I8)
    w0sc_bc = din("w0sc_bc", [128, D], F32)
    b0_bc = din("b0_bc", [128, D], F32)
    wint = din("wint", [NCH * 128, 2048], F32)
    wgt = din("wgt", [NCH * 128, 2048], F32)
    wot = din("wot", [INNER, D], F32)
    convw = din("convw", [128, NCH * 4], F32)
    convb = din("convb", [128, NCH], F32)
    dtw_sel = din("dtw_sel", [128, 64], F32)
    dtb_col = din("dtb_col", [64, 1], F32)
    alog_t = din("alog_t", [128, NCH], F32)
    b_t = din("b_t", [128, NCH], F32)
    c_t = din("c_t", [128, NCH], F32)
    d_t = din("d_t", [128, NCH], F32)
    onesel2 = din("onesel2", [2, 128], F32)
    ones1 = din("ones1", [1, 128], F32)
    ident_bf = din("ident_bf", [128, 128], BF16)
    ident_f32 = din("ident_f32", [128, 128], F32)
    maskadd = din("maskadd", [128, 128], F32)
    sel128 = din("sel128", [128, 512], BF16)
    wq_t = din("wq_t", [16 * 128, 2048], F32)
    wk_t = din("wk_t", [4 * 128, 2048], F32)
    wv_t = din("wv_t", [4 * 128, 2048], F32)
    wo = din("wo", [D, D], F32)
    bq_t = din("bq_t", [128, 16], F32)
    bk_t = din("bk_t", [128, 4], F32)
    bv_t = din("bv_t", [128, 4], F32)
    bo_bc = din("bo_bc", [128, D], F32)
    w1t = din("w1t", [D, D], BF16)
    b1_bc = din("b1_bc", [128, D], BF16)

    st_handles = []
    dbg = {}
    if DBG:
        for nm, sh, dt in [("d_ln", [128, 2048], F32), ("d_qbf", [128, 2048], BF16),
                           ("d_qfm", [128, 2048], BF16), ("d_sm", [128, 64], F32),
                           ("d_st", [128, 64], F32), ("d_po", [128, 2048], F32),
                           ("d_row", [1, 128], F32), ("d_xsws", [128, 8], F32), ("d_cc", [1, 4], F32),
                           ("d_q8", [128, 2048], BF16), ("d_w8", [128, 2048], BF16), ("d_po1", [128, 2048], F32)]:
            dbg[nm] = nc.dram_tensor(nm, sh, dt, kind="ExternalOutput").ap()

    with ExitStack() as ST:
        def sb(name, shape, dt):
            return ST.enter_context(nc.sbuf_tensor(name, shape, dt))

        fm = sb("fm", [128, ND * TOK], BF16)
        fmb = [Buf() for _ in range(ND)]
        stats = sb("stats", [128, 64], F32)
        stats_b = Buf()
        smalls = sb("smalls", [128, 64], F32)
        smalls_b = Buf()
        consts = sb("consts", [128, 4 * NCH + NCH], F32)
        consts_b = Buf()
        mpar = sb("mpar", [128, 3 * NCH], F32)
        mpar_b = Buf()
        dtwsel = sb("dtwsel", [128, 64], F32)
        dtwsel_b = Buf()
        dtbs = sb("dtbs", [64, 1], F32)
        dtbs_b = Buf()
        osel = sb("osel", [2, 128], F32)
        osel_b = Buf()
        ones1s = sb("ones1s", [1, 128], F32)
        ones1s_b = Buf()
        identb = sb("identb", [128, 128], BF16)
        identb_b = Buf()
        identf = sb("identf", [128, 128], F32)
        identf_b = Buf()
        masks = sb("masks", [128, 128], F32)
        masks_b = Buf()
        sel128s = sb("sel128s", [128, 512], BF16)
        sel128s_b = Buf()
        biasq = sb("biasq", [128, 24], F32)
        biasq_b = Buf()
        row1 = sb("row1", [1, 128], F32)
        row1_b = Buf()
        rowt = sb("rowt", [1, 128], F32)
        rowt_b = Buf()
        one1 = sb("one1", [1, 4], F32)
        one1_b = Buf()
        xsws = sb("xsws", [128, 8], F32)
        xsws_b = Buf()

        p.dma("SY", consts[:, 0:4 * NCH], convw, writes=[consts_b])
        p.dma("SY", consts[:, 4 * NCH:5 * NCH], convb, extra=p._deps([consts_b], ()), reads=[consts_b])
        p.dma("SY", dtwsel[:, :], dtw_sel, writes=[dtwsel_b])
        p.dma("SY", dtbs[:, :], dtb_col, writes=[dtbs_b])
        p.dma("SY", osel[:, :], onesel2, writes=[osel_b])
        p.dma("SY", ones1s[:, :], ones1, writes=[ones1s_b])
        p.dma("SY", identb[:, :], ident_bf, writes=[identb_b])
        p.dma("SY", identf[:, :], ident_f32, writes=[identf_b])
        p.dma("SY", masks[:, :], maskadd, writes=[masks_b])
        p.dma("SY", sel128s[:, :], sel128, writes=[sel128s_b])
        p.dma("G", biasq[:, 0:16], bq_t, writes=[biasq_b])
        p.dma("G", biasq[:, 16:20], bk_t, reads=[biasq_b])
        p.dma("G", biasq[:, 20:24], bv_t, reads=[biasq_b])
        p.dma("G", mpar[:, 0:NCH], alog_t, writes=[mpar_b])
        p.dma("G", mpar[:, NCH:2 * NCH], b_t, reads=[mpar_b])
        p.dma("G", mpar[:, 2 * NCH:3 * NCH], c_t, reads=[mpar_b])
        p.op("S", S.activation, mpar[:, 0:NCH], mpar[:, 0:NCH], AF.Exp,
             reads=[mpar_b], writes=[mpar_b])
        p.op("V", V.tensor_scalar, mpar[:, 0:NCH], mpar[:, 0:NCH], -1.0, None, ALU.mult,
             reads=[mpar_b], writes=[mpar_b])
        p.op("V", V.tensor_mul, mpar[:, NCH:2 * NCH], mpar[:, NCH:2 * NCH],
             mpar[:, 2 * NCH:3 * NCH], reads=[mpar_b], writes=[mpar_b])
        p.dma("G", mpar[:, 2 * NCH:3 * NCH], d_t, reads=[mpar_b])

        def stats_tile(i, xa_ap, xa_b, scr_bf, scr_bf_b):
            p.upd("V", V.reduce_sum, stats[:, 32 + i:33 + i], xa_ap, AX.X,
                  reads=[xa_b], writes=[stats_b])
            p.upd("S", S.activation, scr_bf, xa_ap, AF.Square,
                  accum_out=stats[:, 48 + i:49 + i],
                  reads=[xa_b], writes=[stats_b, scr_bf_b])
            p.upd("V", V.tensor_scalar, stats[:, 32 + i:33 + i], stats[:, 32 + i:33 + i],
                  1.0 / D, None, ALU.mult, reads=[stats_b], writes=[stats_b])
            p.upd("V", V.tensor_scalar, stats[:, 48 + i:49 + i], stats[:, 48 + i:49 + i],
                  1.0 / D, None, ALU.mult, reads=[stats_b], writes=[stats_b])
            p.upd("V", V.scalar_tensor_tensor, stats[:, 0 + i:1 + i], stats[:, 32 + i:33 + i],
                  -1.0, stats[:, 32 + i:33 + i], ALU.mult, ALU.mult,
                  reads=[stats_b], writes=[stats_b])
            p.upd("V", V.tensor_add, stats[:, 48 + i:49 + i], stats[:, 48 + i:49 + i],
                  stats[:, 0 + i:1 + i], reads=[stats_b], writes=[stats_b])
            p.upd("V", V.tensor_scalar, stats[:, 48 + i:49 + i], stats[:, 48 + i:49 + i],
                  1e-5, None, ALU.add, reads=[stats_b], writes=[stats_b])
            p.upd("S", S.activation, stats[:, 48 + i:49 + i], stats[:, 48 + i:49 + i],
                  AF.Ln, reads=[stats_b], writes=[stats_b])
            p.upd("S", S.activation, stats[:, 0 + i:1 + i], stats[:, 48 + i:49 + i],
                  AF.Exp, scale=-0.5, reads=[stats_b], writes=[stats_b])
            p.upd("V", V.scalar_tensor_tensor, stats[:, 16 + i:17 + i], stats[:, 32 + i:33 + i],
                  -1.0, stats[:, 0 + i:1 + i], ALU.mult, ALU.mult,
                  reads=[stats_b], writes=[stats_b])

        def ln_apply(i, out_ap, out_b, xa_ap, xa_b):
            p.op("S", S.activation, out_ap, xa_ap, AF.Identity,
                 bias=stats[:, 16 + i:17 + i], scale=stats[:, 0 + i:1 + i],
                 reads=[xa_b, stats_b], writes=[out_b])

        def colmax_to_row1(col_ap, col_b, psc, psc_b):
            p.op("T", T.matmul, psc[0:1, 0:128], col_ap, identf[:, :],
                 start=True, stop=True, reads=[col_b, identf_b], writes=[psc_b])
            p.op("V", V.tensor_copy, rowt[:, :], psc[0:1, 0:128],
                 reads=[psc_b], writes=[rowt_b])
            p.upd("V", V.reduce_max, row1[0:1, 0:1], rowt[:, :], AX.X,
                  reads=[rowt_b], writes=[row1_b])

        def bcast_scalar(src_ap, src_b, dst_col, psc, psc_b):
            p.op("T", T.matmul, psc[:, 0:1], ones1s[:, :], src_ap,
                 start=True, stop=True, reads=[src_b, ones1s_b], writes=[psc_b])
            p.upd("V", V.tensor_copy, xsws[:, dst_col:dst_col + 1], psc[:, 0:1],
                  reads=[psc_b], writes=[xsws_b])

        def src_ap(s, i):
            return (x_in if s == 0 else xres)[128 * i:128 * (i + 1), :]

        def dst_ap(s, i):
            return (y_out if s == n_sub - 1 else xres)[128 * i:128 * (i + 1), :]

        xres_b = [Buf() for _ in range(NT)]

        def build_ffn0(s):
            with ExitStack() as E:
                def lsb(name, shape, dt):
                    return E.enter_context(nc.sbuf_tensor(name, shape, dt))
                xa = [lsb(f"ffn0_xa{u}_{s}", [128, 2048], F32) for u in range(2)]
                xa_b = [Buf(), Buf()]
                ln32 = [lsb(f"ffn0_ln{u}_{s}", [128, 2048], F32) for u in range(2)]
                ln32_b = [Buf(), Buf()]
                qbf = [lsb(f"ffn0_qbf{u}_{s}", [128, 2048], BF16) for u in range(2)]
                qbf_b = [Buf(), Buf()]
                qfm = [lsb(f"ffn0_qfm{u}_{s}", [128, 2048], BF16) for u in range(2)]
                qfm_b = [Buf(), Buf()]
                i8s = lsb(f"ffn0_i8_{s}", [128, 2048], I8)
                i8s_b = Buf()
                bcs = [lsb(f"ffn0_bc{u}_{s}", [128, 2048], F32) for u in range(2)]
                bcs_b = [Buf(), Buf()]
                ef32 = [lsb(f"ffn0_e32{u}_{s}", [128, 512], F32) for u in range(2)]
                ef32_b = [Buf(), Buf()]
                ef16 = [lsb(f"ffn0_e16{u}_{s}", [128, 512], F16) for u in range(2)]
                ef16_b = [Buf(), Buf()]
                smq = [lsb(f"ffn0_smq{u}_{s}", [128, 4], F32) for u in range(2)]
                smq_b = [Buf(), Buf()]
                pt = E.enter_context(nc.psum_tensor(f"ffn0_pt_{s}", [128, 1024], F32))
                ptb2 = [Buf(), Buf()]
                po = E.enter_context(nc.psum_tensor(f"ffn0_po_{s}", [128, 2048], F32))
                po_b = [Buf() for _ in range(4)]
                for j in range(ND):
                    p.dma("SY", i8s[:, :], w0t[128 * j:128 * (j + 1), :], writes=[i8s_b])
                    p.op("V", V.tensor_copy, fm[:, TOK * j:TOK * (j + 1)], i8s[:, :],
                         reads=[i8s_b], writes=[fmb[j]])
                p.dma("SY", bcs[0][:, :], w0sc_bc, writes=[bcs_b[0]])
                p.dma("SY", bcs[1][:, :], b0_bc, writes=[bcs_b[1]])
                for i in range(NT):
                    u = i % 2
                    pr = i % 2
                    p.dma("G", xa[u][:, :], src_ap(s, i), reads=[xres_b[i]], writes=[xa_b[u]])
                    stats_tile(i, xa[u][:, :], xa_b[u], ln32[pr][:, :], ln32_b[pr])
                    ln_apply(i, ln32[pr][:, :], ln32_b[pr], xa[u][:, :], xa_b[u])
                    p.op("V", V.reduce_max, smq[pr][:, 0:1], ln32[pr][:, :], AX.X, True,
                         reads=[ln32_b[pr]], writes=[smq_b[pr]])
                    p.op("V", V.tensor_scalar, smq[pr][:, 1:2], smq[pr][:, 0:1], 1e-7, None,
                         ALU.max, reads=[smq_b[pr]], writes=[smq_b[pr]])
                    p.op("V", V.tensor_scalar, smq[pr][:, 1:2], smq[pr][:, 1:2], INV127, None,
                         ALU.mult, reads=[smq_b[pr]], writes=[smq_b[pr]])
                    p.op("V", V.reciprocal, smq[pr][:, 2:3], smq[pr][:, 1:2],
                         reads=[smq_b[pr]], writes=[smq_b[pr]])
                    p.op("V", V.tensor_scalar, ln32[pr][:, :], ln32[pr][:, :],
                         smq[pr][:, 2:3], None,
                         ALU.mult, reads=[ln32_b[pr], smq_b[pr]], writes=[ln32_b[pr]])
                    p.op("V", V.tensor_scalar, qbf[pr][:, :], ln32[pr][:, :], RND_C, -RND_C,
                         ALU.add, ALU.add, reads=[ln32_b[pr]], writes=[qbf_b[pr]])
                    for g in range(4):
                        sl2 = (i * 4 + g) % 2
                        for jj in range(4):
                            j = 4 * g + jj
                            p.upd("T", T.matmul,
                                  pt[:, 512 * sl2 + 128 * jj:512 * sl2 + 128 * (jj + 1)],
                                  qbf[pr][:, 128 * j:128 * (j + 1)], identb[:, :],
                                  start=True, stop=True,
                                  reads=[qbf_b[pr], identb_b], writes=[ptb2[sl2]])
                        p.upd("S", S.activation, qfm[pr][:, 512 * g:512 * (g + 1)],
                              pt[:, 512 * sl2:512 * (sl2 + 1)],
                              AF.Identity, reads=[ptb2[sl2]], writes=[qfm_b[pr]],
                              extra=(p._deps([], [qfm_b[pr]]) if g == 0 else ()))
                    for j in range(ND):
                        for sl in range(4):
                            p.op("T", T.matmul, po[:, 512 * sl:512 * (sl + 1)],
                                 qfm[pr][:, 128 * j:128 * (j + 1)],
                                 fm[:, TOK * j + 512 * sl:TOK * j + 512 * (sl + 1)],
                                 start=(j == 0), stop=(j == ND - 1),
                                 reads=[qfm_b[pr], fmb[j]], writes=[po_b[sl]])
                    for sl in range(4):
                        sls = slice(512 * sl, 512 * (sl + 1))
                        p.op("V", V.tensor_mul, ef32[pr][:, :], po[:, sls], bcs[0][:, sls],
                             reads=[po_b[sl], bcs_b[0]], writes=[ef32_b[pr]])
                        p.op("V", V.scalar_tensor_tensor, ef16[pr][:, :], ef32[pr][:, :],
                             smq[pr][:, 1:2], bcs[1][:, sls], ALU.mult, ALU.add,
                             reads=[ef32_b[pr], smq_b[pr], bcs_b[1]], writes=[ef16_b[pr]])
                        p.op("V", V.scalar_tensor_tensor, xa[u][:, sls], ef16[pr][:, :], 0.5,
                             xa[u][:, sls], ALU.mult, ALU.add,
                             reads=[ef16_b[pr]], writes=[xa_b[u]])
                    h = p.dma("G", dst_ap(s, i), xa[u][:, :], reads=[xa_b[u]], writes=[xres_b[i]])
                    if s == n_sub - 1:
                        st_handles.append(h)
            p.barrier()

        def build_lnfm(s):
            import os
            LNF = os.environ.get("LNF", "c")
            with ExitStack() as E:
                xa = [E.enter_context(nc.sbuf_tensor(f"lf_xa{u}_{s}", [128, 2048], F32))
                      for u in range(2)]
                xa_b = [Buf(), Buf()]
                lnb = [E.enter_context(nc.sbuf_tensor(f"lf_ln{u}_{s}", [128, 2048], BF16))
                       for u in range(2)]
                lnb_b = [Buf(), Buf()]
                sq32 = E.enter_context(nc.sbuf_tensor(f"lf_sq_{s}", [128, 2048], F32))
                sq32_b = Buf()
                pt = E.enter_context(nc.psum_tensor(f"lf_pt_{s}", [128, 2048], F32))
                pt_b = [Buf() for _ in range(4)]
                for i in range(NT):
                    u = i % 2
                    p.dma("G", xa[u][:, :], src_ap(s, i), reads=[xres_b[i]], writes=[xa_b[u]])
                    stats_tile(i, xa[u][:, :], xa_b[u], sq32[:, :], sq32_b)
                    ln_apply(i, lnb[u][:, :], lnb_b[u], xa[u][:, :], xa_b[u])
                    if LNF == "a":
                        continue
                    for j in range(ND):
                        p.op("T", T.matmul, pt[:, 512 * (j % 4):512 * (j % 4) + 128],
                             lnb[u][:, 128 * j:128 * (j + 1)], identb[:, :],
                             start=True, stop=True,
                             reads=[lnb_b[u], identb_b], writes=[pt_b[j % 4]])
                        if LNF == "b":
                            continue
                        p.upd("S", S.activation,
                              fm[:, TOK * j + 128 * i:TOK * j + 128 * (i + 1)],
                              pt[:, 512 * (j % 4):512 * (j % 4) + 128], AF.Identity,
                              reads=[pt_b[j % 4]], writes=[fmb[j]])
            p.barrier()

        def build_mamba(s):
            import os
            MMPH = os.environ.get("MMPH", "123")
            MM2S = os.environ.get("MM2S", "d")
            build_lnfm(s)
            if "0" == MMPH:
                for i in range(NT):
                    xa0 = wsl32 if False else None
                with ExitStack() as E0:
                    xa = E0.enter_context(nc.sbuf_tensor(f"mm_pass_{s}", [128, 2048], F32))
                    xa_b = Buf()
                    for i in range(NT):
                        p.dma("G", xa[:, :], src_ap(s, i), reads=[xres_b[i]], writes=[xa_b])
                        h = p.dma("G", dst_ap(s, i), xa[:, :], reads=[xa_b], writes=[xres_b[i]])
                        if s == n_sub - 1:
                            st_handles.append(h)
                return
            with ExitStack() as E:
                def lsb(name, shape, dt):
                    return E.enter_context(nc.sbuf_tensor(name, shape, dt))
                wsl = [lsb(f"mm_w{u}", [128, 2048], F32) for u in range(2)]
                wsl_b = [Buf(), Buf()]
                wbb = [lsb(f"mm_wb{u}", [128, 2048], BF16) for u in range(2)]
                wbb_b = [Buf(), Buf()]
                acc = [lsb(f"mm_acc{u}", [128, 2048], F32) for u in range(2)]
                acc_b = [Buf(), Buf()]
                ssc = [lsb(f"mm_ssc{u}", [128, 2048], F32) for u in range(2)]
                ssc_b = [Buf(), Buf()]
                xch = [lsb(f"mm_x{u}", [128, 2048], F32) for u in range(2)]
                xch_b = [Buf(), Buf()]
                tnh = [lsb(f"mm_t{u}", [128, 2048], F32) for u in range(2)]
                tnh_b = [Buf(), Buf()]
                dtraw = lsb("mm_dtraw", [64, 2048], F32)
                dtraw_b = Buf()
                dtfin = lsb("mm_dtfin", [64, 2048], F32)
                dtfin_b = Buf()
                dt2 = [lsb(f"mm_dt2_{u}", [2, 2048], F32) for u in range(2)]
                dt2_b = [Buf(), Buf()]
                dtraw_rows = [Buf() for _ in range(NCH)]

                xscr_b = [Buf() for _ in range(NCH)]
                with (
                    nc.psum_tensor("mm_pp", [128, 2048], F32) as pp,
                    nc.psum_tensor("mm_pdt", [2, 2048], F32) as pdt,
                ):
                    pp_b = Buf()
                    pdt_b = Buf()

                    def emit_dt(kq):
                        uq = kq % 2
                        for sl in range(4):
                            p.op("T", T.matmul, pdt[:, 512 * sl:512 * (sl + 1)],
                                 dtwsel[:, 2 * kq:2 * kq + 2],
                                 xch[uq][:, 512 * sl:512 * (sl + 1)], start=True, stop=True,
                                 reads=[dtwsel_b, xch_b[uq]], writes=[pdt_b])
                        p.op("S", S.activation, dt2[uq][:, :], pdt[:, :],
                             AF.Identity, reads=[pdt_b], writes=[dt2_b[uq]])
                        p.dma("G", dtraw[2 * kq:2 * kq + 2, 0:2048], dt2[uq][:, :],
                              reads=[dt2_b[uq]], writes=[dtraw_rows[kq]])

                    p.dma("SY", wsl[0][:, :], wint[0:128, :], writes=[wsl_b[0]])
                    p.op("V", V.tensor_copy, wbb[0][:, :], wsl[0][:, :],
                         reads=[wsl_b[0]], writes=[wbb_b[0]])
                    for k in range(NCH):
                        u = k % 2
                        for j in range(ND):
                            for sl in range(4):
                                p.op("T", T.matmul, pp[:, 512 * sl:512 * (sl + 1)],
                                     wbb[u][:, 128 * j:128 * (j + 1)],
                                     fm[:, TOK * j + 512 * sl:TOK * j + 512 * (sl + 1)],
                                     start=(j == 0), stop=(j == ND - 1),
                                     reads=[wbb_b[u], fmb[j]], writes=[pp_b])
                        if k + 1 < NCH:
                            p.dma("SY", wsl[1 - u][:, :], wint[128 * (k + 1):128 * (k + 2), :],
                                  writes=[wsl_b[1 - u]])
                            p.op("V", V.tensor_copy, wbb[1 - u][:, :], wsl[1 - u][:, :],
                                 reads=[wsl_b[1 - u]], writes=[wbb_b[1 - u]])
                        if k > 0:
                            emit_dt(k - 1)
                        p.op("S", S.activation, acc[u][:, :], pp[:, :], AF.Identity,
                             scale=consts[:, 4 * k + 3:4 * k + 4],
                             bias=consts[:, 4 * NCH + k:4 * NCH + k + 1],
                             reads=[pp_b, consts_b], writes=[acc_b[u]])
                        for tap in (1, 2, 3):
                            p.op("V", V.scalar_tensor_tensor, acc[u][:, tap:2048],
                                 pp[:, 0:2048 - tap],
                                 consts[:, 4 * k + 3 - tap:4 * k + 4 - tap],
                                 acc[u][:, tap:2048], ALU.mult, ALU.add,
                                 reads=[pp_b, consts_b, acc_b[u]], writes=[acc_b[u]])
                        p.op("S", S.activation, tnh[u][:, :], acc[u][:, :], AF.Tanh, scale=0.5,
                             reads=[acc_b[u]], writes=[tnh_b[u]])
                        p.op("V", V.tensor_scalar, tnh[u][:, :], tnh[u][:, :], 0.5, 0.5,
                             ALU.mult, ALU.add, reads=[tnh_b[u]], writes=[tnh_b[u]])
                        p.op("V", V.tensor_mul, xch[u][:, :], acc[u][:, :], tnh[u][:, :],
                             reads=[acc_b[u], tnh_b[u]], writes=[xch_b[u]])
                        p.dma("G", xscr[128 * k:128 * (k + 1), :], xch[u][:, :],
                              reads=[xch_b[u]], writes=[xscr_b[k]])
                    emit_dt(NCH - 1)
                p.barrier()
                p.op("S", S.activation, dtfin[:, :], dtraw[:, :], AF.Identity,
                     bias=dtbs[:, 0:1], reads=[dtraw_b, dtbs_b] + dtraw_rows,
                     writes=[dtfin_b])
                p.op("S", S.activation, dtraw[:, :], dtfin[:, :], AF.Abs,
                     reads=[dtfin_b], writes=[dtraw_b])
                p.op("S", S.activation, dtraw[:, :], dtraw[:, :], AF.Exp, scale=-1.0,
                     reads=[dtraw_b], writes=[dtraw_b])
                p.op("S", S.activation, dtraw[:, :], dtraw[:, :], AF.Ln, bias=1.0,
                     reads=[dtraw_b], writes=[dtraw_b])
                p.op("V", V.tensor_relu, acc[0][0:64, :], dtfin[:, :],
                     reads=[dtfin_b], writes=[acc_b[0]])
                p.op("V", V.tensor_add, dtfin[:, :], dtraw[:, :], acc[0][0:64, :],
                     reads=[dtraw_b, acc_b[0]], writes=[dtfin_b])
                if "2" not in MMPH:
                    with ExitStack() as E0:
                        xa = E0.enter_context(nc.sbuf_tensor(f"mm_pass2_{s}", [128, 2048], F32))
                        xa_b = Buf()
                        for i in range(NT):
                            p.dma("G", xa[:, :], src_ap(s, i), reads=[xres_b[i]], writes=[xa_b])
                            h = p.dma("G", dst_ap(s, i), xa[:, :], reads=[xa_b], writes=[xres_b[i]])
                            if s == n_sub - 1:
                                st_handles.append(h)
                    return
                comb_b = [Buf() for _ in range(NCH)]
                with (
                    nc.psum_tensor("mm_pg", [128, 2048], F32) as pg,
                    nc.psum_tensor("mm_pb", [128, 2048], F32) as pb,
                ):
                    pg_b = Buf()
                    pb_b = Buf()
                    cmb = [lsb(f"mm_cmb{u}", [128, 2048], BF16) for u in range(2)]
                    cmb_b = [Buf(), Buf()]
                    p.dma("G", dt2[0][:, :], dtfin[0:2, :],
                          reads=[dtfin_b], writes=[dt2_b[0]])
                    p.dma("SY", wsl[0][:, :], wgt[0:128, :], writes=[wsl_b[0]])
                    p.op("V", V.tensor_copy, wbb[0][:, :], wsl[0][:, :],
                         reads=[wsl_b[0]], writes=[wbb_b[0]])
                    p.dma("G", xch[0][:, :], xscr[0:128, :],
                          reads=[xscr_b[0]], writes=[xch_b[0]])
                    for k in range(NCH):
                        u = k % 2
                        if k + 1 < NCH:
                            p.dma("G", dt2[1 - u][:, :], dtfin[2 * k + 2:2 * k + 4, :],
                                  reads=[dtfin_b], writes=[dt2_b[1 - u]])
                            p.dma("SY", wsl[1 - u][:, :], wgt[128 * (k + 1):128 * (k + 2), :],
                                  writes=[wsl_b[1 - u]])
                            p.op("V", V.tensor_copy, wbb[1 - u][:, :], wsl[1 - u][:, :],
                                 reads=[wsl_b[1 - u]], writes=[wbb_b[1 - u]])
                            p.dma("G", xch[1 - u][:, :], xscr[128 * (k + 1):128 * (k + 2), :],
                                  reads=[xscr_b[k + 1]], writes=[xch_b[1 - u]])
                        for j in range(ND):
                            for sl in range(4):
                                p.op("T", T.matmul, pg[:, 512 * sl:512 * (sl + 1)],
                                     wbb[u][:, 128 * j:128 * (j + 1)],
                                     fm[:, TOK * j + 512 * sl:TOK * j + 512 * (sl + 1)],
                                     start=(j == 0), stop=(j == ND - 1),
                                     reads=[wbb_b[u], fmb[j]], writes=[pg_b])
                        for sl in range(4):
                            p.op("T", T.matmul, pb[:, 512 * sl:512 * (sl + 1)],
                                 osel[:, :], dt2[u][:, 512 * sl:512 * (sl + 1)],
                                 start=True, stop=True,
                                 reads=[osel_b, dt2_b[u]], writes=[pb_b])
                        if MM2S == "a":
                            p.op("S", S.activation, acc[u][:, :], pg[:, :], AF.Identity,
                                 reads=[pg_b], writes=[acc_b[u]])
                            p.op("S", S.activation, acc[u][:, :], pb[:, :], AF.Identity,
                                 reads=[pb_b], writes=[acc_b[u]])
                            continue
                        p.op("S", S.activation, acc[u][:, :], pb[:, :], AF.Exp,
                             scale=mpar[:, k:k + 1], reads=[pb_b, mpar_b], writes=[acc_b[u]])
                        p.op("S", S.activation, ssc[u][:, :], pb[:, :], AF.Identity,
                             reads=[pb_b], writes=[ssc_b[u]])
                        p.op("V", V.scalar_tensor_tensor, tnh[u][:, :], xch[u][:, :],
                             mpar[:, NCH + k:NCH + k + 1], ssc[u][:, :], ALU.mult, ALU.mult,
                             reads=[xch_b[u], mpar_b, ssc_b[u]], writes=[tnh_b[u]])
                        if MM2S == "b":
                            continue
                        p.op("V", V.tensor_tensor_scan, ssc[u][:, :], acc[u][:, :],
                             tnh[u][:, :], 0.0, ALU.mult, ALU.add,
                             reads=[acc_b[u], tnh_b[u]], writes=[ssc_b[u]])
                        p.op("V", V.scalar_tensor_tensor, tnh[u][:, :], xch[u][:, :],
                             mpar[:, 2 * NCH + k:2 * NCH + k + 1], ssc[u][:, :],
                             ALU.mult, ALU.add,
                             reads=[xch_b[u], mpar_b, ssc_b[u]], writes=[tnh_b[u]])
                        if MM2S == "c":
                            continue
                        p.op("S", S.activation, acc[u][:, :], pg[:, :], AF.Tanh, scale=0.5,
                             reads=[pg_b], writes=[acc_b[u]])
                        p.op("V", V.tensor_scalar, acc[u][:, :], acc[u][:, :], 0.5, None,
                             ALU.mult, reads=[acc_b[u]], writes=[acc_b[u]])
                        p.op("V", V.tensor_scalar, acc[u][:, :], acc[u][:, :], 0.5, None,
                             ALU.add, reads=[acc_b[u]], writes=[acc_b[u]])
                        p.op("V", V.tensor_mul, cmb[u][:, :], tnh[u][:, :], acc[u][:, :],
                             reads=[tnh_b[u], acc_b[u]], writes=[cmb_b[u]])
                        p.dma("G", combscr[128 * k:128 * (k + 1), :], cmb[u][:, :],
                              reads=[cmb_b[u]], writes=[comb_b[k]])
                if "3" not in MMPH:
                    with ExitStack() as E0:
                        xa = E0.enter_context(nc.sbuf_tensor(f"mm_pass3_{s}", [128, 2048], F32))
                        xa_b = Buf()
                        for i in range(NT):
                            p.dma("G", xa[:, :], src_ap(s, i), reads=[xres_b[i]], writes=[xa_b])
                            h = p.dma("G", dst_ap(s, i), xa[:, :], reads=[xa_b], writes=[xres_b[i]])
                            if s == n_sub - 1:
                                st_handles.append(h)
                    return
                cstrip = [lsb(f"mm_cs{u}", [128, 2048], BF16) for u in range(2)]
                cs_b = [Buf(), Buf()]
                with nc.psum_tensor("mm_po", [128, 2048], F32) as po:
                    po_b = [Buf() for _ in range(4)]
                    for half in range(2):
                        for j in range(ND):
                            kk = 16 * half + j
                            p.dma("SY", wsl[j % 2][:, :], wot[128 * kk:128 * (kk + 1), :],
                                  writes=[wsl_b[j % 2]])
                            p.op("V", V.tensor_copy, fm[:, TOK * j:TOK * (j + 1)],
                                 wsl[j % 2][:, :], reads=[wsl_b[j % 2]], writes=[fmb[j]])
                        for i in range(NT):
                            u = i % 2
                            src = combscr[2048 * half:2048 * (half + 1),
                                          128 * i:128 * (i + 1)]
                            p.dma("G", cstrip[u][:, :].rearrange("p (j c) -> p j c", c=128),
                                  src.rearrange("(j p) c -> p j c", p=128),
                                  reads=[comb_b[16 * half + jj] for jj in range(ND)],
                                  writes=[cs_b[u]])
                            for j in range(ND):
                                for sl in range(4):
                                    p.op("T", T.matmul, po[:, 512 * sl:512 * (sl + 1)],
                                         cstrip[u][:, 128 * j:128 * (j + 1)],
                                         fm[:, TOK * j + 512 * sl:TOK * j + 512 * (sl + 1)],
                                         start=(j == 0), stop=(j == ND - 1),
                                         reads=[cs_b[u], fmb[j]], writes=[po_b[sl]])
                            if half == 0:
                                p.op("S", S.activation, acc[i % 2][:, :], po[:, :], AF.Identity,
                                     reads=[po_b[0], po_b[1], po_b[2], po_b[3]],
                                     writes=[acc_b[i % 2]])
                                p.dma("G", xscr[128 * i:128 * (i + 1), :], acc[i % 2][:, :],
                                      reads=[acc_b[i % 2]], writes=[xscr_b[i]])
                            else:
                                u = i % 2
                                p.dma("G", xch[u][:, :], xscr[128 * i:128 * (i + 1), :],
                                      reads=[xscr_b[i]], writes=[xch_b[u]])
                                p.dma("SY", acc[u][:, :], src_ap(s, i),
                                      reads=[xres_b[i]], writes=[acc_b[u]])
                                p.op("V", V.tensor_add, xch[u][:, :], xch[u][:, :], po[:, :],
                                     reads=[xch_b[u], po_b[0], po_b[1], po_b[2], po_b[3]],
                                     writes=[xch_b[u]])
                                p.op("V", V.tensor_add, xch[u][:, :], xch[u][:, :],
                                     acc[u][:, :], reads=[xch_b[u], acc_b[u]],
                                     writes=[xch_b[u]])
                                h = p.dma("G", dst_ap(s, i), xch[u][:, :],
                                          reads=[xch_b[u]], writes=[xres_b[i]])
                                if s == n_sub - 1:
                                    st_handles.append(h)
            p.barrier()

        def build_ffn1(s, callno):
            with ExitStack() as E:
                def lsb(name, shape, dt):
                    return E.enter_context(nc.sbuf_tensor(name, shape, dt))
                xa = [lsb(f"f1_xa{u}_{callno}", [128, 2048], F32) for u in range(2)]
                xa_b = [Buf(), Buf()]
                ln32 = lsb(f"f1_ln_{callno}", [128, 2048], F32)
                ln32_b = Buf()
                qbf = lsb(f"f1_qbf_{callno}", [128, 2048], BF16)
                qbf_b = Buf()
                qfm = lsb(f"f1_qfm_{callno}", [128, 2048], BF16)
                qfm_b = Buf()
                f8s = lsb(f"f1_f8_{callno}", [128, 2048], F8)
                f8s_b = Buf()
                bcb = lsb(f"f1_bcb_{callno}", [128, 2048], BF16)
                bcb_b = Buf()
                ebf = lsb(f"f1_ebf_{callno}", [128, 512], BF16)
                ebf_b = Buf()
                pt = E.enter_context(nc.psum_tensor(f"f1_pt_{callno}", [128, 512], F32))
                pt_b = [Buf() for _ in range(4)]
                po = E.enter_context(nc.psum_tensor(f"f1_po_{callno}", [128, 2048], F32))
                po_b = [Buf() for _ in range(4)]
                psc = E.enter_context(nc.psum_tensor(f"f1_psc_{callno}", [128, 128], F32))
                psc_b = Buf()
                p.dma("SY", bcb[:, :], b1_bc, writes=[bcb_b])
                for j in range(ND):
                    p.dma("SY", fm[:, TOK * j:TOK * (j + 1)], w1t[128 * j:128 * (j + 1), :],
                          writes=[fmb[j]])
                    p.upd("V", V.reduce_max, smalls[:, 8 + j:9 + j],
                          fm[:, TOK * j:TOK * (j + 1)], AX.X, True,
                          reads=[fmb[j]], writes=[smalls_b])
                p.op("V", V.reduce_max, smalls[:, 24:25], smalls[:, 8:24], AX.X,
                     reads=[smalls_b], writes=[smalls_b])
                colmax_to_row1(smalls[:, 24:25], smalls_b, psc, psc_b)
                p.upd("V", V.tensor_scalar_max, row1[0:1, 0:1], row1[0:1, 0:1], 1e-6,
                      reads=[row1_b], writes=[row1_b])
                p.upd("V", V.tensor_copy, row1[0:1, 1:2], row1[0:1, 0:1],
                      reads=[row1_b], writes=[row1_b])
                p.upd("V", V.reciprocal, row1[0:1, 2:3], row1[0:1, 1:2],
                      reads=[row1_b], writes=[row1_b])
                bcast_scalar(row1[0:1, 2:3], row1_b, 0, psc, psc_b)
                for j in range(ND):
                    p.op("V", V.tensor_scalar, f8s[:, :], fm[:, TOK * j:TOK * (j + 1)],
                         xsws[:, 0:1], None, ALU.mult,
                         reads=[fmb[j], xsws_b], writes=[f8s_b])
                    p.op("V", V.tensor_copy, fm[:, TOK * j:TOK * (j + 1)], f8s[:, :],
                         reads=[f8s_b], writes=[fmb[j]])
                for i in range(NT):
                    u = i % 2
                    p.dma("G", xa[u][:, :], src_ap(s, i), reads=[xres_b[i]], writes=[xa_b[u]])
                    stats_tile(i, xa[u][:, :], xa_b[u], ln32[:, :], ln32_b)
                    ln_apply(i, ln32[:, :], ln32_b, xa[u][:, :], xa_b[u])
                    p.upd("V", V.reduce_max, smalls[:, 8 + i:9 + i], ln32[:, :], AX.X, True,
                          reads=[ln32_b], writes=[smalls_b])
                p.op("V", V.reduce_max, smalls[:, 24:25], smalls[:, 8:24], AX.X,
                     reads=[smalls_b], writes=[smalls_b])
                colmax_to_row1(smalls[:, 24:25], smalls_b, psc, psc_b)
                p.upd("V", V.tensor_scalar_max, row1[0:1, 0:1], row1[0:1, 0:1], 1e-6,
                      reads=[row1_b], writes=[row1_b])
                p.upd("V", V.memset, row1[0:1, 3:7], 0.0, reads=[row1_b], writes=[row1_b])
                p.upd("V", V.tensor_copy, row1[0:1, 3:4], row1[0:1, 0:1],
                      reads=[row1_b], writes=[row1_b])
                p.dma("SY", cc_in[callno], row1[0:1, 3:7], reads=[row1_b])
                p.op("G", G.collective_compute, "AllReduce", ALU.max,
                     [[0, 1, 2, 3, 4, 5, 6, 7]], [cc_in[callno]], [cc_out[callno]],
                     extra=[Hd(p.sems["SY"], p.cnt["SY"])])
                p.dma("SY", row1[0:1, 3:7], cc_out[callno],
                      extra=[Hd(p.sems["G"], p.cnt["G"])], writes=[row1_b])
                p.upd("V", V.reciprocal, row1[0:1, 4:5], row1[0:1, 3:4],
                      reads=[row1_b], writes=[row1_b])
                p.upd("V", V.tensor_mul, row1[0:1, 5:6], row1[0:1, 3:4], row1[0:1, 1:2],
                      reads=[row1_b], writes=[row1_b])
                bcast_scalar(row1[0:1, 4:5], row1_b, 1, psc, psc_b)
                bcast_scalar(row1[0:1, 5:6], row1_b, 2, psc, psc_b)
                if DBG and callno == 0:
                    p.dma("G", dbg["d_row"], row1[:, :], reads=[row1_b])
                    p.dma("G", dbg["d_xsws"], xsws[:, :], reads=[xsws_b])
                    p.dma("G", dbg["d_cc"], cc_in[callno], extra=[Hd(p.sems["G"], p.cnt["G"])])
                for i in range(NT):
                    u = i % 2
                    p.dma("G", xa[u][:, :], src_ap(s, i), reads=[xres_b[i]], writes=[xa_b[u]])
                    ln_apply(i, ln32[:, :], ln32_b, xa[u][:, :], xa_b[u])
                    p.op("V", V.tensor_scalar, f8s[:, :], ln32[:, :], xsws[:, 1:2], None,
                         ALU.mult, reads=[ln32_b, xsws_b], writes=[f8s_b])
                    p.op("V", V.tensor_copy, qbf[:, :], f8s[:, :],
                         reads=[f8s_b], writes=[qbf_b])
                    for g in range(4):
                        for jj in range(4):
                            j = 4 * g + jj
                            p.op("T", T.matmul, pt[:, 128 * jj:128 * (jj + 1)],
                                 qbf[:, 128 * j:128 * (j + 1)], identb[:, :],
                                 start=True, stop=True,
                                 reads=[qbf_b, identb_b], writes=[pt_b[jj]])
                        p.op("S", S.activation, qfm[:, 512 * g:512 * (g + 1)], pt[:, :],
                             AF.Identity, reads=[pt_b[0], pt_b[1], pt_b[2], pt_b[3]],
                             writes=[qfm_b])
                    for j in range(ND):
                        for sl in range(4):
                            p.op("T", T.matmul, po[:, 512 * sl:512 * (sl + 1)],
                                 qfm[:, 128 * j:128 * (j + 1)],
                                 fm[:, TOK * j + 512 * sl:TOK * j + 512 * (sl + 1)],
                                 start=(j == 0), stop=(j == ND - 1),
                                 reads=[qfm_b, fmb[j]], writes=[po_b[sl]])
                    if DBG and callno == 0 and i == 0:
                        p.dma("G", dbg["d_q8"], qbf[:, :], reads=[qbf_b])
                        p.dma("G", dbg["d_w8"], fm[:, 0:TOK], reads=[fmb[0]])
                        dpo2 = nc.sbuf_tensor(f"d_po2_sb_{callno}", [128, 2048], F32)
                        with dpo2 as dpo2_t:
                            dpo2_b = Buf()
                            p.op("S", S.activation, dpo2_t[:, :], po[:, :], AF.Identity,
                                 reads=[po_b[0], po_b[1], po_b[2], po_b[3]], writes=[dpo2_b])
                            p.dma("G", dbg["d_po1"], dpo2_t[:, :], reads=[dpo2_b])
                    for sl in range(4):
                        sls = slice(512 * sl, 512 * (sl + 1))
                        p.op("S", S.activation, ebf[:, :], po[:, sls], AF.Identity,
                             scale=xsws[:, 2:3], reads=[po_b[sl], xsws_b], writes=[ebf_b])
                        p.op("V", V.tensor_add, ebf[:, :], ebf[:, :], bcb[:, sls],
                             reads=[ebf_b, bcb_b], writes=[ebf_b])
                        p.op("V", V.scalar_tensor_tensor, xa[u][:, sls], ebf[:, :], 0.5,
                             xa[u][:, sls], ALU.mult, ALU.add,
                             reads=[ebf_b], writes=[xa_b[u]])
                    h = p.dma("G", dst_ap(s, i), xa[u][:, :], reads=[xa_b[u]], writes=[xres_b[i]])
                    if s == n_sub - 1:
                        st_handles.append(h)
            p.barrier()

        def build_attn(s):
            build_lnfm(s)
            with ExitStack() as E:
                def lsb(name, shape, dt):
                    return E.enter_context(nc.sbuf_tensor(name, shape, dt))
                qfm = lsb("at_qfm", [128, 16 * 2048], BF16)
                qfm_b = [Buf() for _ in range(16)]
                kfm = lsb("at_kfm", [128, 4 * 2048], BF16)
                kfm_b = [Buf() for _ in range(4)]
                vfm = lsb("at_vfm", [128, 4 * 2048], BF16)
                vfm_b = [Buf() for _ in range(4)]
                E2 = E.enter_context(ExitStack())
                def lsb2(name, shape, dt):
                    return E2.enter_context(nc.sbuf_tensor(name, shape, dt))
                wsl = [lsb2(f"at_w{u}", [128, 2048], F32) for u in range(2)]
                wsl_b = [Buf(), Buf()]
                enb = lsb2("at_enb0", [128, 2048], BF16)
                enb_b = Buf()
                with nc.psum_tensor("at_pq", [128, 2048], F32) as pq:
                    pq_b = [Buf() for _ in range(4)]
                    for (wt, nchunk, dstarena, dst_b, btcol) in (
                        (wq_t, 16, qfm, qfm_b, 0), (wk_t, 4, kfm, kfm_b, 16),
                        (wv_t, 4, vfm, vfm_b, 20),
                    ):
                        for cch in range(nchunk):
                            u = cch % 2
                            p.dma("SY", wsl[u][:, :], wt[128 * cch:128 * (cch + 1), :],
                                  writes=[wsl_b[u]])
                            p.op("V", V.tensor_copy, enb[:, :], wsl[u][:, :],
                                 reads=[wsl_b[u]], writes=[enb_b])
                            for j in range(ND):
                                for sl in range(4):
                                    p.op("T", T.matmul, pq[:, 512 * sl:512 * (sl + 1)],
                                         enb[:, 128 * j:128 * (j + 1)],
                                         fm[:, TOK * j + 512 * sl:TOK * j + 512 * (sl + 1)],
                                         start=(j == 0), stop=(j == ND - 1),
                                         reads=[enb_b, fmb[j]], writes=[pq_b[sl]])
                            p.op("S", S.activation, dstarena[:, 2048 * cch:2048 * (cch + 1)],
                                 pq[:, :], AF.Identity,
                                 bias=biasq[:, btcol + cch:btcol + cch + 1],
                                 reads=[pq_b[0], pq_b[1], pq_b[2], pq_b[3], biasq_b],
                                 writes=[dst_b[cch]])
                p.barrier()
                E2.close()
                E3 = E.enter_context(ExitStack())
                def lsb3(name, shape, dt):
                    return E3.enter_context(nc.sbuf_tensor(name, shape, dt))
                krep = lsb3("at_krep", [128, 2048], BF16)
                krep_b = Buf()
                vrep = lsb3("at_vrep", [128, 2048], BF16)
                vrep_b = Buf()
                enb2 = [lsb3(f"at_enbp{u}", [128, 2048], BF16) for u in range(2)]
                enb2_b = [Buf(), Buf()]
                sm2 = [lsb3(f"at_sm{u}", [128, 2], F32) for u in range(2)]
                sm2_b = [Buf(), Buf()]
                ent = lsb3("at_ent", [128, 1024], BF16)
                entb = [Buf(), Buf()]
                pttb = [Buf(), Buf()]
                with (
                    nc.psum_tensor("at_ps", [128, 2048], F32) as ps,
                    nc.psum_tensor("at_pt", [128, 1024], F32) as ptt,
                    nc.psum_tensor("at_po", [128, 1024], F32) as poh,
                ):
                    ps_b = Buf()
                    ptt_b = Buf()
                    poh_b = Buf()
                    for h in range(16):
                        g = h // 4
                        r = h % 4
                        for sl in range(4):
                            p.op("T", T.matmul, ptt[:, 0:512],
                                 sel128s[:, 128 * r:128 * (r + 1)],
                                 kfm[:, 2048 * g + 512 * sl:2048 * g + 512 * (sl + 1)],
                                 start=True, stop=True,
                                 reads=[sel128s_b, kfm_b[g]],
                                 writes=[ptt_b],
                                 extra=(p._deps([pttb[0], pttb[1]], []) if sl == 0 else ()))
                            p.upd("S", S.activation, krep[:, 512 * sl:512 * (sl + 1)],
                                  ptt[:, 0:512], AF.Identity, reads=[ptt_b], writes=[krep_b])
                        for gq in range(4):
                            for jj in range(4):
                                tkc = 4 * gq + jj
                                p.op("T", T.matmul, ptt[:, 128 * jj:128 * (jj + 1)],
                                     vfm[:, 2048 * g + 128 * tkc:2048 * g + 128 * (tkc + 1)],
                                     sel128s[:, 128 * r:128 * (r + 1)],
                                     start=True, stop=True,
                                     reads=[vfm_b[g], sel128s_b], writes=[ptt_b])
                            p.upd("S", S.activation, vrep[:, 512 * gq:512 * (gq + 1)],
                                  ptt[:, 0:512], AF.Identity, reads=[ptt_b],
                                  writes=[vrep_b] if gq == 3 else [],
                                  extra=p._deps([], [vrep_b]) if gq == 0 else ())
                        for half in range(2):
                            for i in range(8 * half, 8 * half + 8):
                                ncol = (i + 1) * 128
                                nsl = (ncol + 511) // 512
                                pr = i % 2
                                for sl in range(nsl):
                                    w = min(512, ncol - 512 * sl)
                                    p.op("T", T.matmul, ps[:, 512 * sl:512 * sl + w],
                                         qfm[:, 2048 * h + 128 * i:2048 * h + 128 * (i + 1)],
                                         krep[:, 512 * sl:512 * sl + w],
                                         start=True, stop=True,
                                         reads=[qfm_b[h], krep_b], writes=[ps_b])
                                p.op("V", V.tensor_add, ps[:, 128 * i:128 * (i + 1)],
                                     ps[:, 128 * i:128 * (i + 1)], masks[:, :],
                                     reads=[ps_b, masks_b], writes=[ps_b],
                                     extra=p._deps([ps_b], []))
                                h_exp = p.op("S", S.activation, enb2[pr][:, 0:ncol],
                                             ps[:, 0:ncol], AF.Exp, scale=SM_SCALE,
                                             accum_out=sm2[pr][:, 0:1],
                                             reads=[ps_b], writes=[enb2_b[pr]],
                                             extra=p._deps([], [sm2_b[pr]]))
                                sm2_b[pr].w = [h_exp]
                                sm2_b[pr].r = []
                                p.upd("V", V.reciprocal, sm2[pr][:, 1:2], sm2[pr][:, 0:1],
                                      reads=[sm2_b[pr]], writes=[sm2_b[pr]])
                                p.op("V", V.tensor_scalar, enb2[pr][:, 0:ncol],
                                     enb2[pr][:, 0:ncol], sm2[pr][:, 1:2], None, ALU.mult,
                                     reads=[enb2_b[pr], sm2_b[pr]], writes=[enb2_b[pr]])
                                for gq in range((i + 4) // 4):
                                    ue = gq % 2
                                    t0 = 4 * gq
                                    nt = min(4, i + 1 - t0)
                                    for tt in range(nt):
                                        p.upd("T", T.matmul,
                                              ptt[:, 512 * ue + 128 * tt:512 * ue + 128 * (tt + 1)],
                                              enb2[pr][:, 128 * (t0 + tt):128 * (t0 + tt + 1)],
                                              identb[:, :], start=True, stop=True,
                                              reads=[enb2_b[pr], identb_b], writes=[pttb[ue]])
                                    p.op("S", S.activation, ent[:, 512 * ue:512 * ue + 128 * nt],
                                         ptt[:, 512 * ue:512 * ue + 128 * nt], AF.Identity,
                                         reads=[pttb[ue]], writes=[entb[ue]])
                                    for tt in range(nt):
                                        tkc = t0 + tt
                                        p.op("T", T.matmul,
                                             poh[:, 128 * (i - 8 * half):128 * (i - 8 * half + 1)],
                                             vrep[:, 128 * tkc:128 * (tkc + 1)],
                                             ent[:, 512 * ue + 128 * tt:512 * ue + 128 * (tt + 1)],
                                             start=(tkc == 0), stop=(tkc == i),
                                             reads=[vrep_b, entb[ue]], writes=[poh_b])
                            p.upd("S", S.activation,
                                  fm[:, TOK * h + 1024 * half:TOK * h + 1024 * (half + 1)],
                                  poh[:, :], AF.Identity, reads=[poh_b], writes=[fmb[h]])
                p.barrier()
                E3.close()
                E4 = E.enter_context(ExitStack())
                def lsb4(name, shape, dt):
                    return E4.enter_context(nc.sbuf_tensor(name, shape, dt))
                wsl = [lsb4(f"at_ww{u}", [128, 2048], F32) for u in range(2)]
                wsl_b = [Buf(), Buf()]
                xa = [lsb4(f"at_xa{u}", [128, 2048], F32) for u in range(2)]
                xa_b = [Buf(), Buf()]
                bcs = lsb4("at_bc", [128, 2048], F32)
                bcs_b = Buf()
                p.dma("SY", bcs[:, :], bo_bc, writes=[bcs_b])
                for rc in range(16):
                    u = rc % 2
                    p.dma("SY", wsl[u][:, :], wo[128 * rc:128 * (rc + 1), :],
                          writes=[wsl_b[u]])
                    p.op("V", V.tensor_copy, qfm[:, 2048 * rc:2048 * (rc + 1)], wsl[u][:, :],
                         reads=[wsl_b[u]], writes=[qfm_b[rc]])
                with nc.psum_tensor("at_pw", [128, 2048], F32) as pw:
                    pw_b = [Buf() for _ in range(4)]
                    for i in range(NT):
                        u = i % 2
                        p.dma("G", xa[u][:, :], src_ap(s, i), reads=[xres_b[i]],
                              writes=[xa_b[u]])
                        for rc in range(16):
                            for sl in range(4):
                                p.op("T", T.matmul, pw[:, 512 * sl:512 * (sl + 1)],
                                     fm[:, TOK * rc + 128 * i:TOK * rc + 128 * (i + 1)],
                                     qfm[:, 2048 * rc + 512 * sl:2048 * rc + 512 * (sl + 1)],
                                     start=(rc == 0), stop=(rc == 15),
                                     reads=[fmb[rc], qfm_b[rc]], writes=[pw_b[sl]])
                        for sl in range(4):
                            sls = slice(512 * sl, 512 * (sl + 1))
                            p.upd("V", V.tensor_add, xa[u][:, sls], xa[u][:, sls], pw[:, sls],
                                  reads=[pw_b[sl], xa_b[u]], writes=[xa_b[u]])
                            p.upd("V", V.tensor_add, xa[u][:, sls], xa[u][:, sls], bcs[:, sls],
                                  reads=[bcs_b, xa_b[u]], writes=[xa_b[u]])
                        h2 = p.dma("G", dst_ap(s, i), xa[u][:, :], reads=[xa_b[u]],
                                   writes=[xres_b[i]])
                        if s == n_sub - 1:
                            st_handles.append(h2)
            p.barrier()

        builders = [
            lambda s: build_ffn0(s),
            lambda s: build_mamba(s),
            lambda s: build_ffn0(s),
            lambda s: build_ffn1(s, 0),
            lambda s: build_attn(s),
            lambda s: build_ffn1(s, 1),
        ]
        for sidx in range(n_sub):
            builders[sidx](sidx)

        p.finish(st_handles)
    p.close()
    return nc


def _host_layout(inputs):
    f32 = np.float32
    g = lambda k: np.asarray(inputs[k])
    x = g("L0_attn_hidden_states")
    w0 = g("L0_ffn_weight"); w0sc = g("L0_ffn_w_scale"); b0 = g("L0_ffn_bias")
    win = g("L0_attn_in_proj_w"); wg = g("L0_attn_gate_proj_w")
    wout = g("L0_attn_out_proj_w")
    cw = g("L0_attn_conv_weight"); cb = g("L0_attn_conv_bias")
    dtw = g("L0_attn_dt_w"); dtb = g("L0_attn_dt_b")
    alog = g("L0_attn_A_log"); B = g("L0_attn_B"); C = g("L0_attn_C"); Dp = g("L0_attn_D")
    wq = g("L1_attn_Wq"); bq = g("L1_attn_bq")
    wk = g("L1_attn_Wk"); bk = g("L1_attn_bk")
    wv = g("L1_attn_Wv"); bv = g("L1_attn_bv")
    wo = g("L1_attn_Wo"); bo = g("L1_attn_bo")
    w1 = g("L1_ffn_weight"); b1 = g("L1_ffn_bias")
    ca = np.ascontiguousarray

    def slab_tiles(w_T_like, nchunk):
        d_in, n_out = w_T_like.shape
        nj = d_in // 128
        a = w_T_like.reshape(nj, 128, nchunk, 128)
        a = a.transpose(2, 1, 0, 3).reshape(nchunk * 128, nj * 128)
        return ca(a)

    common = {
        "w0t": ca(w0.T),
        "w0sc_bc": ca(np.broadcast_to(w0sc[:, 0][None, :], (128, D))).astype(f32),
        "b0_bc": ca(np.broadcast_to(b0[None, :], (128, D))).astype(f32),
        "wint": slab_tiles(win.T, NCH),
        "wgt": slab_tiles(wg.T, NCH),
        "wot": ca(wout.T),
        "convw": ca(cw.reshape(NCH, 128, 4).transpose(1, 0, 2).reshape(128, NCH * 4)),
        "convb": ca(cb.reshape(NCH, 128).T),
        "dtb_col": ca(dtb[:, None]).astype(f32),
        "alog_t": ca(alog.reshape(-1).reshape(NCH, 128).T),
        "b_t": ca(B.reshape(-1).reshape(NCH, 128).T),
        "c_t": ca(C.reshape(-1).reshape(NCH, 128).T),
        "d_t": ca(Dp.reshape(-1).reshape(NCH, 128).T),
        "wq_t": slab_tiles(wq, 16),
        "wk_t": slab_tiles(wk, 4),
        "wv_t": slab_tiles(wv, 4),
        "wo": ca(wo),
        "bq_t": ca(bq.reshape(16, 128).T), "bk_t": ca(bk.reshape(4, 128).T),
        "bv_t": ca(bv.reshape(4, 128).T),
        "bo_bc": ca(np.broadcast_to(bo[None, :], (128, D))).astype(f32),
        "w1t": ca(w1.T), "b1_bc": ca(np.broadcast_to(b1[None, :], (128, D))),
    }
    dts = np.zeros((128, 64), f32)
    for k in range(NCH):
        dts[0:64, 2 * k] = dtw[2 * k, :]
        dts[64:128, 2 * k + 1] = dtw[2 * k + 1, :]
    common["dtw_sel"] = dts
    ons = np.zeros((2, 128), f32); ons[0, 0:64] = 1; ons[1, 64:128] = 1
    common["onesel2"] = ons
    common["ones1"] = np.ones((1, 128), f32)
    common["ident_bf"] = np.eye(128).astype(ml_dtypes.bfloat16)
    common["ident_f32"] = np.eye(128, dtype=f32)
    mk = np.zeros((128, 128), f32)
    mk[np.triu_indices(128, 1)] = -1e30
    common["maskadd"] = mk
    s128 = np.zeros((4, 128, 128), f32)
    for r in range(4):
        for hd in range(128):
            s128[r, r * 32 + hd // 4, hd] = 1.0
    common["sel128"] = ca(s128.transpose(1, 0, 2).reshape(128, 512)).astype(ml_dtypes.bfloat16)
    maps = []
    for c in range(8):
        m = dict(common)
        m["x"] = ca(x[c])
        maps.append(m)
    return maps


def kernel(**inputs):
    maps = _host_layout(inputs)
    nc = build_nc()
    res = run_bass_kernel_spmd(nc, maps, core_ids=list(range(8)))
    out = np.stack([np.asarray(r["y"]) for r in res.results], axis=0)
    return np.ascontiguousarray(out.astype(np.float32))
